# version 02
# baseline (speedup 1.0000x reference)
import numpy as np
import ml_dtypes
from contextlib import ExitStack

import concourse.bass as bass
import concourse.mybir as mybir
from concourse.bass_utils import run_bass_kernel_spmd

F32 = mybir.dt.float32
BF16 = mybir.dt.bfloat16
AF = mybir.ActivationFunctionType
ALU = mybir.AluOpType
AX = mybir.AxisListType.X

B_LOC = 2
T = 2048
D = 1024
SEG = 64
S_SEG = 32
C = 2
CT = C * 128
FFT = 2
EPS_A = 1e-6
EPS_LN = 1e-5


CHAINED = {
    "ln1s": "dve", "ln1n": "dve", "sgd": "dve", "sig": "dve", "vev": "dve",
    "msk": "dve", "wrd": "dve", "dnr": "dve", "o2": "dve", "mzs": "dve",
    "l2s": "dve", "l2n": "dve", "rsd": "dve", "bp2": "dve", "bp4": "dve",
    "bet": "dve", "f3s": "dve", "f3n": "dve", "mix": "dve", "kae": "dve",
    "rev": "dve", "ffm": "dve", "f4s": "dve", "ffo": "dve", "p1d": "dve",
    "ln1r": "act", "ln1": "act", "sge": "act", "qke": "act", "exp": "act",
    "o1": "act", "l2r": "act", "aoc": "act", "wtr": "act", "f3r": "act",
    "ln3": "act", "kre": "act", "fsg": "act", "f4r": "act", "bp1": "act",
    "bp3": "act",
}


class Env:

    def __init__(self, sems):
        self.sems = sems
        self.counts = {}
        self.chains = {"dve": 0, "act": 0}
        self.events = {}
        self.mode = "count"

    def reset(self):
        self.counts = {}
        self.chains = {"dve": 0, "act": 0}
        self.mode = "emit"

    def fire(self, sem_name, event, instr=None, dma=False):
        if sem_name in CHAINED:
            ch = CHAINED[sem_name]
            if self.mode == "count":
                self.events[(sem_name, event)] = (ch, self.chains[ch])
            return
        if instr is None:
            v = self.counts.get(sem_name, 0)
            if self.mode == "count":
                self.events[(sem_name, event)] = (sem_name, v)
            return v
        inc = 16 if dma else 1
        v = self.counts.get(sem_name, 0) + inc
        self.counts[sem_name] = v
        if self.mode == "count":
            self.events[(sem_name, event)] = (sem_name, v)
        if self.mode == "emit":
            instr.then_inc(self.sems[sem_name], inc)
        return v

    def wait(self, eng, sem_name, event, optional=False):
        if self.mode == "count":
            return
        key = (sem_name, event)
        if key not in self.events:
            if optional:
                return
            raise KeyError(f"unknown event {key}")
        actual, v = self.events[key]
        if v > 0:
            eng.wait_ge(self.sems[actual], v)


class ChainEng:
    _PASS = {"wait_ge", "nop", "engine_nop", "then_inc", "dma_start"}

    def __init__(self, eng, E, key):
        self.eng = eng
        self.E = E
        self.key = key

    def __getattr__(self, name):
        raw = getattr(self.eng, name)
        if name in self._PASS or not callable(raw):
            return raw
        E = self.E
        key = self.key

        def call(*a, **kw):
            n = E.chains[key]
            if E.mode == "emit" and n > 0:
                self.eng.wait_ge(E.sems[key], n)
            instr = raw(*a, **kw)
            if E.mode == "emit":
                instr.then_inc(E.sems[key], 1)
            E.chains[key] = n + 1
            return instr

        return call


def build_nc(n_chunks=S_SEG // C, do_ffn=True, debug=0, x2_out=False):
    NCH = n_chunks
    NU = NCH * C * 8
    NST = NCH * C
    NFF = (B_LOC * T) // (FFT * 128) if do_ffn else 0
    NFT = NFF * FFT
    NWT = NCH * C

    nc = bass.Bass("TRN2", target_bir_lowering=False)

    x_in = nc.dram_tensor("x", [B_LOC, T, D], F32, kind="ExternalInput").ap()
    w_hbm = {wn: nc.dram_tensor(wn, [D, D], F32, kind="ExternalInput").ap()
             for wn in ["wq", "wk", "wv", "wo", "wr", "wfk", "wfv"]}
    beta_in = nc.dram_tensor("beta", [1, 16], F32, kind="ExternalInput").ap()
    mur_in = nc.dram_tensor("mur", [1, D], F32, kind="ExternalInput").ap()
    muk_in = nc.dram_tensor("muk", [1, D], F32, kind="ExternalInput").ap()
    mask_in = nc.dram_tensor("mask01", [128, 128], F32, kind="ExternalInput").ap()
    iden_in = nc.dram_tensor("iden", [128, 128], F32, kind="ExternalInput").ap()
    out_hbm = nc.dram_tensor("out", [B_LOC, T, D], F32, kind="ExternalOutput").ap()
    x2_hbm = nc.dram_tensor("x2scr", [B_LOC, T, D], F32,
                            kind="ExternalOutput" if x2_out else "Internal").ap()
    mb_hbm = nc.dram_tensor("mbscr", [2, 64, 512], BF16, kind="Internal").ap()

    dbg = {}
    if debug:
        for name, cols in [("d_xln1", C * 1024), ("d_qfm", 8 * CT), ("d_kfm", 8 * CT),
                           ("d_sqf", 8 * CT), ("d_skf", 8 * CT), ("d_sktm", C * 1024),
                           ("d_vtm", C * 1040), ("d_atm", C * 1024), ("d_mz", 2080),
                           ("d_x2", C * 1024), ("d_f_kact", 2048), ("d_f_rtm", 2048),
                           ("d_f_xr", 2048), ("d_f_xk", 2048), ("d_f_xln3", 2048),
                           ("d_f_fsc", 1024), ("d_f_x3fm", 2176)]:
            dbg[name] = nc.dram_tensor(name, [128, cols], F32, kind="ExternalOutput").ap()

    ctx = ExitStack()
    sb = lambda n, sh, dt_: ctx.enter_context(nc.sbuf_tensor(n, sh, dt_))
    sem_ = lambda n: ctx.enter_context(nc.semaphore(n))

    wt = {k: sb(f"wt_{k}", [128, 8 * 1024], BF16) for k in ["q", "k", "v", "o"]}
    wstage = [sb(f"wstage{i}", [128, 1024], BF16) for i in range(2)]
    iden = sb("idensb", [128, 128], BF16)
    mask01 = sb("masksb", [128, 128], BF16)
    brow = sb("brow", [1, 128], F32)
    bsrc = sb("bsrc", [1, 64], F32)
    bcast = sb("bcast", [128, 64], F32)
    zrow = sb("zrow", [1, 128], BF16)
    mu_fm = sb("mu_fm", [128, 16], F32)
    cst = sb("cst", [128, 4], F32)
    zmast = sb("zmast", [128, 16], F32)
    zinc = sb("zinc", [128, 16], F32)
    mz2 = sb("mz2", [128, 16 * 130], BF16)
    mstg = sb("mstg", [128, 512], BF16)

    xch = [sb(f"xch{i}", [128, C * 1024], F32) for i in range(2)]
    xln1 = [sb(f"xln1_{i}", [128, C * 1024], BF16) for i in range(2)]
    xfm = [sb(f"xfm{i}", [128, 8 * CT], BF16) for i in range(2)]
    qfm = [sb(f"qfm{i}", [128, 8 * CT], BF16) for i in range(2)]
    kfm = [sb(f"kfm{i}", [128, 8 * CT], BF16) for i in range(2)]
    sqf = [sb(f"sqf{i}", [128, 8 * CT], BF16) for i in range(2)]
    skf = [sb(f"skf{i}", [128, 8 * CT], BF16) for i in range(2)]
    _scra = sb("scra", [128, 8 * CT], BF16)
    _scrb = sb("scrb", [128, 8 * CT], BF16)
    scra = [_scra, _scra]
    scrb = [_scrb, _scrb]
    sktm = [sb(f"sktm{i}", [128, C * 1024], BF16) for i in range(2)]
    vtm = [sb(f"vtm{i}", [128, C * 1040], BF16) for i in range(2)]
    atm = [sb(f"atm{i}", [128, C * 1024], BF16) for i in range(2)]
    afm = [sb(f"afm{i}", [128, 8 * CT], BF16) for i in range(2)]
    x2ch = [sb(f"x2ch{i}", [128, C * 1024], F32) for i in range(2)]
    aosc = [sb(f"aosc{i}", [128, 1024], BF16) for i in range(2)]
    ptsb = [sb(f"ptsb{i}", [128, 256], BF16) for i in range(2)]
    wsb = [sb(f"wsb{i}", [128, 128], BF16) for i in range(2)]
    t1sb = [sb(f"t1sb{i}", [128, 128], BF16) for i in range(2)]
    t2sb = [sb(f"t2sb{i}", [128, 128], BF16) for i in range(2)]
    rsc = [sb(f"rsc{i}", [128, 12], F32) for i in range(2)]
    lnst = [sb(f"lnst{i}", [128, 24], F32) for i in range(2)]
    l2st = [sb(f"l2st{i}", [128, 16], F32) for i in range(2)]
    f4st = [sb(f"f4st{i}", [128, 32], F32) for i in range(2)]

    x2f, xln3, xrfm, xkfm, kact, rtm, fsig, fsc, fout = (
        xch, xln1, qfm, kfm, sqf, skf, scra, aosc, x2ch)
    x3fm_s = sb("x3fm", [128, 8 * 272], BF16)
    x3fm = [x3fm_s, x3fm_s]
    csv = sb("csv", [128, 8], BF16)

    pbank = [ctx.enter_context(nc.psum_tensor(f"pb{i}", [128, 512], F32)) for i in range(8)]
    PRJ = [pbank[0], pbank[1]]
    MBK = [pbank[2], pbank[3]]

    def _slot(u, i):
        return pbank[4 + 2 * (u % 2) + i]

    def pt_ap(u, par):
        return _slot(u, par)[:, 0:128]

    def amem_ap(u, b):
        return _slot(u, b)[:, 128:258]

    def vprev_ap(u, b):
        return _slot(u, b)[:, 258:388]

    def adot_ap(u, par):
        return _slot(u, par)[:, 388:453]

    def mtile_ap(c, par, b):
        return MBK[c // 4][64 * b:64 * (b + 1),
                           (c % 4) * 128 + par * 64:(c % 4) * 128 + par * 64 + 64]

    sem_names = ["xin0", "xin1", "wst0", "wst1", "mub", "trx0", "trx1", "trk0", "trk1",
                 "tra0", "tra1", "x2o0", "x2o1", "x2i0", "x2i1", "out0", "out1", "trl",
                 "mbl0", "mbl1",
                 "mbs", "ln1s", "ln1r", "ln1n", "ln1", "sgd", "sge", "sig",
                 "qke", "vev", "bp1", "bp2", "bp3", "bp4", "bt5", "bet",
                 "exp", "msk", "wrd", "dnr", "mzs", "o1", "o2",
                 "l2s", "l2r", "l2n", "aoc", "rsd",
                 "pmm", "avm", "adm", "dmm", "qpe", "wpe", "wtm", "wtr",
                 "f3s", "f3r", "f3n", "ln3", "mix", "kpe", "kre", "kae",
                 "rpe", "fsg", "rev", "vp2", "ffm", "f4s", "f4r", "ffo", "p1d", "dve", "act"]
    sems = {n: sem_(n) for n in sem_names}
    E = Env(sems)

    def fmcol(ic, sl, b=0, j0=0):
        return ic * CT + sl * 128 + b * 64 + j0

    def uidx(g, sl, c):
        return g * (C * 8) + sl * 8 + c

    WL_A = [("wq", wt["q"]), ("wk", wt["k"]), ("wv", wt["v"]), ("wo", wt["o"])]
    WL_F = [("wr", wt["q"]), ("wfk", wt["k"]), ("wfv", wt["v"])]
    NSA = 32


    def emit_gpsimd(g):
        i = g.dma_start(out=brow[0:1, 0:16], in_=beta_in[:, :]); E.fire("mub", "beta", i, dma=True)
        i = g.dma_start(out=mu_fm[:, 0:8], in_=mur_in[0, :].rearrange("(ic p) -> p ic", p=128),
                        allow_slow_non_contiguous=True); E.fire("mub", "mur", i, dma=True)
        i = g.dma_start(out=mu_fm[:, 8:16], in_=muk_in[0, :].rearrange("(ic p) -> p ic", p=128),
                        allow_slow_non_contiguous=True); E.fire("mub", "muk", i, dma=True)
        i = g.dma_start(out=mask01[:], in_=mask_in[:, :]); E.fire("mub", "mask", i, dma=True)
        i = g.dma_start(out=iden[:], in_=iden_in[:, :]); E.fire("mub", "iden", i, dma=True)
        E.fire("mub", "all")

        E.wait(g, "bp4", 0)
        i = g.dma_start(out=bcast[0:1, :], in_=bsrc[0:1, :]); E.fire("bt5", "d0", i, dma=True)
        r = 1
        k = 0
        while r < 128:
            E.wait(g, "bt5", f"d{k}")
            n = min(r, 128 - r)
            i = g.dma_start(out=bcast[r:r + n, :], in_=bcast[0:n, :])
            k += 1
            E.fire("bt5", f"d{k}", i, dma=True)
            r += n
        E.fire("bt5", "done")

        for wi, (wn, _) in enumerate(WL_A):
            for rr in range(8):
                idx = wi * 8 + rr
                if idx >= 2:
                    E.wait(g, "wtm", ("stage", idx - 2))
                i = g.dma_start(out=wstage[idx % 2][:],
                                in_=w_hbm[wn][128 * rr:128 * (rr + 1), :])
                E.fire(f"wst{idx % 2}", ("st", idx), i, dma=True)

        def x_load(gc):
            if gc >= 2:
                E.wait(g, "rsd", ("w", 2 * (gc - 2) + 1))
            for b in range(B_LOC):
                i = g.dma_start(
                    out=xch[gc % 2][:].rearrange("p (s n) -> p s n", n=1024)[64 * b:64 * (b + 1), :, :],
                    in_=x_in[b, 128 * gc:128 * (gc + 1), :].rearrange("(s j) d -> j s d", j=64))
                E.fire(f"xin{gc % 2}", ("x", gc, b), i, dma=True)
            E.fire(f"xin{gc % 2}", ("chunk", gc))

        def bounce(t):
            E.wait(g, "mzs", ("t", t))
            for half in range(2):
                i = g.dma_start(out=mb_hbm[half, :, :], in_=mstg[64 * half:64 * (half + 1), :])
                E.fire("mbs", ("s", t, half), i, dma=True)
            E.fire("mbs", ("t", t))
            E.wait(g, "mbs", ("t", t))
            mzv = mz2[:].rearrange("p (c two n) -> p c two n", two=2, n=130)
            i = g.dma_start(out=mzv[64:128, :, 0, 64:128],
                            in_=mb_hbm[0, :, :].rearrange("p (cc n) -> p cc n", n=64))
            E.fire(f"mbl{t % 2}", ("l", t, 0), i, dma=True)
            i = g.dma_start(out=mzv[0:64, :, 1, 0:64],
                            in_=mb_hbm[1, :, :].rearrange("p (cc n) -> p cc n", n=64))
            E.fire(f"mbl{t % 2}", ("l", t, 1), i, dma=True)
            E.fire(f"mbl{t % 2}", ("t", t))

        def x2_store(gc):
            E.wait(g, "rsd", ("w", 2 * gc + 1))
            for b in range(B_LOC):
                i = g.dma_start(
                    out=x2_hbm[b, 128 * gc:128 * (gc + 1), :].rearrange("(s j) d -> j s d", j=64),
                    in_=x2ch[gc % 2][:].rearrange("p (s n) -> p s n", n=1024)[64 * b:64 * (b + 1), :, :])
                E.fire(f"x2o{gc % 2}", ("s", gc, b), i, dma=True)
            E.fire(f"x2o{gc % 2}", ("chunk", gc))

        for gc in range(NCH):
            x_load(gc)
            if gc >= 1:
                for sl in range(C):
                    bounce((gc - 1) * C + sl)
            if gc >= 2:
                x2_store(gc - 2)
        for sl in range(C):
            bounce((NCH - 1) * C + sl)
        if NCH >= 2:
            x2_store(NCH - 2)
        x2_store(NCH - 1)

        if do_ffn:
            for wi, (wn, _) in enumerate(WL_F):
                for rr in range(8):
                    idx = NSA + wi * 8 + rr
                    E.wait(g, "wtm", ("stage", idx - 2))
                    if wi == 0 and rr == 0:
                        E.wait(g, "qpe", ("chunk", NCH - 1))
                    i = g.dma_start(out=wstage[idx % 2][:],
                                    in_=w_hbm[wn][128 * rr:128 * (rr + 1), :])
                    E.fire(f"wst{idx % 2}", ("st", idx), i, dma=True)
            def x2_load(f):
                if f >= 2:
                    E.wait(g, "ffo", ("ft", (f - 2) * FFT + FFT - 1))
                else:
                    E.wait(g, "rsd", ("w", NWT - 1))
                b = (f * FFT * 128) // T
                t0 = (f * FFT * 128) % T
                need = (t0 + FFT * 128) // 128
                E.wait(g, f"x2o{(need - 1) % 2}", ("chunk", need - 1))
                if need >= 2:
                    E.wait(g, f"x2o{(need - 2) % 2}", ("chunk", need - 2))
                alias_g = NCH - 2 + (f % 2)
                E.wait(g, f"x2o{alias_g % 2}", ("chunk", alias_g))
                i = g.dma_start(
                    out=x2f[f % 2][:].rearrange("p (t n) -> p t n", n=1024),
                    in_=x2_hbm[b, t0:t0 + FFT * 128, :].rearrange("(t p) d -> p t d", p=128))
                E.fire(f"x2i{f % 2}", ("f", f), i, dma=True)

            def out_store(ft):
                E.wait(g, "ffo", ("ft", ft))
                b = (ft * 128) // T
                t0 = (ft * 128) % T
                i = g.dma_start(out=out_hbm[b, t0:t0 + 128, :],
                                in_=fout[(ft // FFT) % 2][:, 1024 * (ft % FFT):1024 * (ft % FFT + 1)])
                E.fire(f"out{(ft // FFT) % 2}", ("ft", ft), i, dma=True)

            for f in range(NFF):
                x2_load(f)
                if f >= 1:
                    for t in range(FFT):
                        out_store((f - 1) * FFT + t)
            for t in range(FFT):
                out_store((NFF - 1) * FFT + t)

        if debug:
            if debug >= 1 and NCH >= 1:
                E.wait(g, "sig", ("g", 0))
                E.wait(g, "trk0", ("g", 0))
                E.wait(g, "vev", ("g", 0))
                E.wait(g, "ln1", ("g", 0))
                for nm, src in [("d_xln1", xln1[0]), ("d_qfm", qfm[0]), ("d_kfm", kfm[0]),
                                ("d_sqf", sqf[0]), ("d_skf", skf[0]), ("d_sktm", sktm[0]),
                                ("d_vtm", vtm[0])]:
                    i = g.dma_start(out=dbg[nm], in_=src[:])
                    E.fire("out0", ("dbg", nm), i, dma=True)
            if debug >= 2:
                E.wait(g, "o2", ("u", C * 8 - 1))
                E.wait(g, f"mbl{(C - 1) % 2}", ("t", C - 1))
                i = g.dma_start(out=dbg["d_atm"], in_=atm[0][:]); E.fire("out0", ("dbg", "d_atm"), i, dma=True)
                i = g.dma_start(out=dbg["d_mz"], in_=mz2[:]); E.fire("out0", ("dbg", "d_mz"), i, dma=True)
            if debug >= 3:
                E.wait(g, "rsd", ("w", C - 1))
                i = g.dma_start(out=dbg["d_x2"], in_=x2ch[0][:]); E.fire("out0", ("dbg", "d_x2"), i, dma=True)
            if debug >= 4 and do_ffn:
                E.wait(g, "ffo", ("ft", NFT - 1))
                for nm, src_ap in [("d_f_kact", kact[1][:]), ("d_f_rtm", rtm[1][:]),
                                   ("d_f_xr", xrfm[1][:]), ("d_f_xk", xkfm[1][:]),
                                   ("d_f_xln3", xln3[1][:]), ("d_f_fsc", fsc[1][:]),
                                   ("d_f_x3fm", x3fm_s[:])]:
                    i = g.dma_start(out=dbg[nm], in_=src_ap)
                    E.fire("out0", ("dbg", nm), i, dma=True)

        if E.mode == "emit":
            for so in ("out0", "out1", "out"):
                if E.counts.get(so, 0) > 0:
                    g.wait_ge(sems[so], E.counts[so])
        if not do_ffn:
            E.wait(g, "rsd", ("w", NWT - 1))

    def emit_sync(s):
        E.wait(s, "bt5", "done")
        for gc in range(NCH):
            E.wait(s, "ln1", ("g", gc))
            if gc >= 2:
                E.wait(s, "qpe", ("chunk", gc - 2))
            for sl in range(C):
                for ic in range(8):
                    i = s.dma_start(out=xfm[gc % 2][:, fmcol(ic, sl):fmcol(ic, sl) + 128],
                                    in_=xln1[gc % 2][:, sl * 1024 + ic * 128:sl * 1024 + (ic + 1) * 128],
                                    transpose=True)
                    E.fire(f"trx{gc % 2}", ("d", gc, sl, ic), i, dma=True)
            E.fire(f"trx{gc % 2}", ("g", gc))
            E.wait(s, "sig", ("g", gc))
            if gc >= 2:
                E.wait(s, "dmm", ("u", uidx(gc - 2, C - 1, 7)))
            for sl in range(C):
                for ic in range(8):
                    i = s.dma_start(out=sktm[gc % 2][:, sl * 1024 + ic * 128:sl * 1024 + (ic + 1) * 128],
                                    in_=skf[gc % 2][:, fmcol(ic, sl):fmcol(ic, sl) + 128],
                                    transpose=True)
                    E.fire(f"trk{gc % 2}", ("d", gc, sl, ic), i, dma=True)
            E.fire(f"trk{gc % 2}", ("g", gc))

    def emit_tensor(te):
        bank_free = {}

        def use_bank(bid, free_list):
            for (sn, ev) in bank_free.get(bid, []):
                E.wait(te, sn, ev)
            bank_free[bid] = free_list

        def w_transpose(idx):
            E.wait(te, f"wst{idx % 2}", ("st", idx))
            for ic in range(8):
                gseq = idx * 8 + ic
                use_bank(gseq % 2, [("wtr", ("e", gseq))])
                i = te.matmul(PRJ[gseq % 2][:, 0:128], wstage[idx % 2][:, 128 * ic:128 * (ic + 1)],
                              iden[:], start=True, stop=True)
                E.fire("wtm", ("mm", gseq), i)
            E.fire("wtm", ("stage", idx))

        E.wait(te, "mub", "all")
        for idx in range(NSA):
            w_transpose(idx)

        E.wait(te, "bet", 0)
        for mb_i in range(2):
            for c4 in range(4):
                te.matmul(MBK[mb_i][:, 128 * c4:128 * (c4 + 1)], zrow[0:1, :], zrow[0:1, :],
                          start=(c4 == 0), stop=(c4 == 3))

        def proj_qk(gc, wtt, dst_kind):
            E.wait(te, f"trx{gc % 2}", ("g", gc))
            for oc in range(8):
                use_bank(oc % 2, [("qke", ("e", gc, dst_kind, oc))])
                i = None
                for ic in range(8):
                    i = te.matmul(PRJ[oc % 2][:, 0:CT],
                                  wtt[:, ic * 1024 + oc * 128:ic * 1024 + oc * 128 + 128],
                                  xfm[gc % 2][:, ic * CT:(ic + 1) * CT],
                                  start=(ic == 0), stop=(ic == 7))
                E.fire("qpe", ("t", gc, dst_kind, oc), i)

        def proj_v(gc):
            for sl in range(C):
                for h2 in range(2):
                    j = 2 * sl + h2
                    use_bank(h2, [("vev", ("e", gc, j))])
                    i = None
                    for ic in range(8):
                        i = te.matmul(PRJ[h2][:, 0:512],
                                      xfm[gc % 2][:, fmcol(ic, sl):fmcol(ic, sl) + 128],
                                      wt["v"][:, ic * 1024 + h2 * 512:ic * 1024 + (h2 + 1) * 512],
                                      start=(ic == 0), stop=(ic == 7))
                    E.fire("qpe", ("v", gc, j), i)
            E.fire("qpe", ("chunk", gc))

        def wo_chunk(gc):
            E.wait(te, f"tra{gc % 2}", ("g", gc))
            for t in range(C):
                w = C * gc + t
                for h2 in range(2):
                    use_bank(h2, [("l2s", ("w", w)), ("aoc", ("w", w))])
                    i = None
                    for ic in range(8):
                        i = te.matmul(PRJ[h2][:, 0:512],
                                      afm[gc % 2][:, fmcol(ic, t):fmcol(ic, t) + 128],
                                      wt["o"][:, ic * 1024 + h2 * 512:ic * 1024 + (h2 + 1) * 512],
                                      start=(ic == 0), stop=(ic == 7))
                    E.fire("wpe", ("h", w, h2), i)
                E.fire("wpe", ("w", w))
            E.fire("wpe", ("chunk", gc))

        def scan_unit(gc, sl, c):
            u = uidx(gc, sl, c)
            t = gc * C + sl
            ring = gc % 2
            E.wait(te, "sig", ("g", gc))
            if u >= 2:
                E.wait(te, "o2", ("u", u - 2))
            i = None
            for par in range(2):
                i = te.matmul(pt_ap(u, par),
                              kfm[ring][64 * par:64 * (par + 1), fmcol(c, sl):fmcol(c, sl) + 128],
                              qfm[ring][64 * par:64 * (par + 1), fmcol(c, sl):fmcol(c, sl) + 128],
                              start=True, stop=True, tile_position=(64 * par, 0))
            E.fire("pmm", ("u", u), i)
            if t >= 1:
                E.wait(te, "mzs", ("t", t - 1))
                E.wait(te, f"mbl{(t - 1) % 2}", ("t", t - 1))
            for b in range(2):
                te.matmul(amem_ap(u, b), sqf[ring][:, fmcol(c, sl):fmcol(c, sl) + 128],
                          mz2[:, (2 * c + b) * 130:(2 * c + b + 1) * 130],
                          start=True, stop=True)
            i = None
            for b in range(2):
                i = te.matmul(vprev_ap(u, b), skf[ring][:, fmcol(c, sl):fmcol(c, sl) + 128],
                              mz2[:, (2 * c + b) * 130:(2 * c + b + 1) * 130],
                              start=True, stop=True)
            E.fire("avm", ("u", u), i)
            E.wait(te, "msk", ("u", u))
            E.wait(te, "p1d", ("u", u))
            E.wait(te, "vev", ("g", gc))
            i = None
            for par in range(2):
                i = te.matmul(adot_ap(u, par), ptsb[u % 2][:, 128 * par:128 * (par + 1)],
                              vtm[ring][:, sl * 1040 + c * 130 + par * 65:sl * 1040 + c * 130 + (par + 1) * 65],
                              start=True, stop=True)
            E.fire("adm", ("u", u), i)
            E.wait(te, f"trk{gc % 2}", ("g", gc))
            if t >= 1 and c == 0:
                E.wait(te, "mzs", ("t", t - 1))
            i = None
            for par in range(2):
                for b in range(2):
                    i = te.matmul(mtile_ap(c, par, b),
                                  sktm[ring][64 * b:64 * (b + 1),
                                             sl * 1024 + c * 128 + par * 64:sl * 1024 + c * 128 + (par + 1) * 64],
                                  wsb[u % 2][64 * b:64 * (b + 1), par * 64:(par + 1) * 64],
                                  start=False, stop=False, skip_group_check=True,
                                  tile_position=(64 * b, 64 * b))
            E.fire("dmm", ("u", u), i)

        for gc in range(NCH):
            if gc >= 2:
                wo_chunk(gc - 2)
            proj_qk(gc, wt["q"], "q")
            proj_qk(gc, wt["k"], "k")
            proj_v(gc)
            if gc >= 1:
                for sl in range(C):
                    for c in range(8):
                        scan_unit(gc - 1, sl, c)
        for sl in range(C):
            for c in range(8):
                scan_unit(NCH - 1, sl, c)
        if NCH >= 2:
            wo_chunk(NCH - 2)
        wo_chunk(NCH - 1)

        if do_ffn:
            for idx in range(NSA, NSA + 24):
                w_transpose(idx)

            for f in range(NFF):
                E.wait(te, "mix", ("f", f))
                for oc in range(8):
                    use_bank(oc % 2, [("kre", ("e", f, oc))])
                    i = None
                    for ic in range(8):
                        i = te.matmul(PRJ[oc % 2][:, 0:FFT * 128],
                                      wt["k"][:, ic * 1024 + oc * 128:ic * 1024 + oc * 128 + 128],
                                      xkfm[f % 2][:, ic * FFT * 128:(ic + 1) * FFT * 128],
                                      start=(ic == 0), stop=(ic == 7))
                    E.fire("kpe", ("t", f, oc), i)
                E.fire("kpe", ("f", f))
                for t in range(FFT):
                    ft = f * FFT + t
                    for h2 in range(2):
                        use_bank(h2, [("fsg", ("h", ft, h2))])
                        i = None
                        for ic in range(8):
                            i = te.matmul(PRJ[h2][:, 0:512],
                                          xrfm[f % 2][:, ic * FFT * 128 + t * 128:ic * FFT * 128 + (t + 1) * 128],
                                          wt["q"][:, ic * 1024 + h2 * 512:ic * 1024 + (h2 + 1) * 512],
                                          start=(ic == 0), stop=(ic == 7))
                        E.fire("rpe", ("h", ft, h2), i)
                    E.fire("rpe", ("ft", ft))
                E.wait(te, "kae", ("f", f))
                for t in range(FFT):
                    ft = f * FFT + t
                    for h2 in range(2):
                        use_bank(h2, [("ffm", ("h", ft, h2))])
                        i = None
                        for ic in range(8):
                            i = te.matmul(PRJ[h2][:, 0:512],
                                          kact[f % 2][:, ic * FFT * 128 + t * 128:ic * FFT * 128 + (t + 1) * 128],
                                          wt["v"][:, ic * 1024 + h2 * 512:ic * 1024 + (h2 + 1) * 512],
                                          start=(ic == 0), stop=(ic == 7))
                        E.fire("vp2", ("h", ft, h2), i)
                    E.fire("vp2", ("ft", ft))

    def emit_scalar(sc):
        E.wait(sc, "mub", "all")
        i = sc.activation(brow[0:1, 16:32], brow[0:1, 0:16], AF.Exp, scale=-1.0)
        E.fire("bp1", 0, i)
        E.wait(sc, "bp2", 0)
        i = sc.activation(brow[0:1, 64:80], brow[0:1, 48:64], AF.Ln)
        E.fire("bp3", 0, i)

        for idx in range(NSA):
            wname, dst = WL_A[idx // 8]
            rr = idx % 8
            for ic in range(8):
                gseq = idx * 8 + ic
                E.wait(sc, "wtm", ("mm", gseq))
                i = sc.copy(dst[:, ic * 1024 + 128 * rr:ic * 1024 + 128 * (rr + 1)],
                            PRJ[gseq % 2][:, 0:128])
                E.fire("wtr", ("e", gseq), i)

        def ln1_act(gc):
            st = lnst[gc % 2]
            E.wait(sc, "ln1s", ("g", gc))
            sc.activation(st[:, 16:18], st[:, 14:16], AF.Ln, bias=cst[:, 0:1])
            i = sc.activation(st[:, 18:20], st[:, 16:18], AF.Exp, scale=-0.5)
            E.fire("ln1r", ("g", gc), i)
            E.wait(sc, "ln1n", ("g", gc))
            if gc >= 2:
                E.wait(sc, f"trx{gc % 2}", ("g", gc - 2))
            i = None
            for sl in range(C):
                i = sc.activation(xln1[gc % 2][:, sl * 1024:(sl + 1) * 1024],
                                  xch[gc % 2][:, sl * 1024:(sl + 1) * 1024], AF.Identity,
                                  bias=st[:, 20 + sl:21 + sl], scale=st[:, 18 + sl:19 + sl])
            E.fire("ln1", ("g", gc), i)

        def qk_evac(gc):
            for kind, dstb in [("q", qfm), ("k", kfm)]:
                for oc in range(8):
                    E.wait(sc, "qpe", ("t", gc, kind, oc))
                    if gc >= 2:
                        E.wait(sc, "dmm", ("u", uidx(gc - 2, C - 1, 7)))
                    i = sc.copy(dstb[gc % 2][:, oc * CT:(oc + 1) * CT], PRJ[oc % 2][:, 0:CT])
                    E.fire("qke", ("e", gc, kind, oc), i)
            E.fire("qke", ("g", gc))

        def sig_exp(gc):
            E.wait(sc, "sgd", ("g", gc))
            sc.activation(scra[gc % 2][:], scra[gc % 2][:], AF.Exp)
            i = sc.activation(scrb[gc % 2][:], scrb[gc % 2][:], AF.Exp)
            E.fire("sge", ("g", gc), i)

        def unit_act(gc, sl, c):
            u = uidx(gc, sl, c)
            E.wait(sc, "avm", ("u", u))
            if u >= 2:
                E.wait(sc, "adm", ("u", u - 2))
            i = None
            for par in range(2):
                i = sc.activation(ptsb[u % 2][:, 128 * par:128 * (par + 1)],
                                  pt_ap(u, par), AF.Exp, scale=0.125,
                                  bias=bcast[:, 16 + 2 * c + par:17 + 2 * c + par])
            E.fire("exp", ("u", u), i)
            E.wait(sc, "dnr", ("u", u))
            if u >= 2:
                E.wait(sc, "o2", ("u", u - 2))
            i = None
            for par in range(2):
                i = sc.activation(t1sb[u % 2][:, 64 * par:64 * (par + 1)],
                                  adot_ap(u, par)[:, 0:64], AF.Identity,
                                  scale=rsc[u % 2][:, 8 + par:9 + par])
            E.fire("o1", ("u", u), i)

        def tra_issue(gp):
            E.wait(sc, "o2", ("u", uidx(gp, C - 1, 7)))
            if gp >= 2:
                E.wait(sc, "wpe", ("chunk", gp - 2))
            for sl in range(C):
                for ic in range(8):
                    i = sc.dma_start(out=afm[gp % 2][:, fmcol(ic, sl):fmcol(ic, sl) + 128],
                                     in_=atm[gp % 2][:, sl * 1024 + ic * 128:sl * 1024 + (ic + 1) * 128],
                                     transpose=True)
                    E.fire(f"tra{gp % 2}", ("d", gp, sl, ic), i, dma=True)
            E.fire(f"tra{gp % 2}", ("g", gp))

        def wo_act(gw):
            for t in range(C):
                w = C * gw + t
                st = l2st[w % 2]
                E.wait(sc, "l2s", ("w", w))
                sc.activation(st[:, 8:9], st[:, 4:5], AF.Ln, bias=cst[:, 0:1])
                i = sc.activation(st[:, 9:10], st[:, 8:9], AF.Exp, scale=-0.5)
                E.fire("l2r", ("w", w), i)
                E.wait(sc, "l2n", ("w", w))
                if w >= 2:
                    E.wait(sc, "rsd", ("w", w - 2))
                i = None
                for h2 in range(2):
                    i = sc.activation(aosc[w % 2][:, 512 * h2:512 * (h2 + 1)],
                                      PRJ[h2][:, 0:512], AF.Identity,
                                      bias=st[:, 10:11], scale=st[:, 9:10])
                E.fire("aoc", ("w", w), i)

        for gc in range(NCH):
            if gc >= 2:
                wo_act(gc - 2)
            ln1_act(gc)
            qk_evac(gc)
            sig_exp(gc)
            if gc >= 1:
                for sl in range(C):
                    for c in range(8):
                        unit_act(gc - 1, sl, c)
                tra_issue(gc - 1)
        for sl in range(C):
            for c in range(8):
                unit_act(NCH - 1, sl, c)
        tra_issue(NCH - 1)
        if NCH >= 2:
            wo_act(NCH - 2)
        wo_act(NCH - 1)

        if do_ffn:
            for idx in range(NSA, NSA + 24):
                wname, dst = WL_F[(idx - NSA) // 8]
                rr = idx % 8
                for ic in range(8):
                    gseq = idx * 8 + ic
                    E.wait(sc, "wtm", ("mm", gseq))
                    i = sc.copy(dst[:, ic * 1024 + 128 * rr:ic * 1024 + 128 * (rr + 1)],
                                PRJ[gseq % 2][:, 0:128])
                    E.fire("wtr", ("e", gseq), i)
            for f in range(NFF):
                st = f4st[f % 2]
                E.wait(sc, "f3s", ("f", f))
                sc.activation(st[:, 16:18], st[:, 14:16], AF.Ln, bias=cst[:, 0:1])
                i = sc.activation(st[:, 18:20], st[:, 16:18], AF.Exp, scale=-0.5)
                E.fire("f3r", ("f", f), i)
                E.wait(sc, "f3n", ("f", f))
                if f >= 2:
                    E.wait(sc, "trl", ("f", f - 2))
                else:
                    E.wait(sc, f"trx{(NCH - 2 + (f % 2)) % 2}", ("g", NCH - 2 + (f % 2)))
                i = None
                for t in range(FFT):
                    i = sc.activation(xln3[f % 2][:, t * 1024:(t + 1) * 1024],
                                      x2f[f % 2][:, t * 1024:(t + 1) * 1024], AF.Identity,
                                      bias=st[:, 20 + t:21 + t], scale=st[:, 18 + t:19 + t])
                E.fire("ln3", ("f", f), i)
                E.wait(sc, "ln3", ("f", f))
                if f >= 1:
                    E.wait(sc, "mix", ("f", f - 1))
                for t in range(FFT):
                    for ic in range(8):
                        i = sc.dma_start(
                            out=x3fm_s[:, ic * 272 + 16 + t * 128:ic * 272 + 16 + (t + 1) * 128],
                            in_=xln3[f % 2][:, t * 1024 + ic * 128:t * 1024 + (ic + 1) * 128],
                            transpose=True)
                        E.fire("trl", ("d", f, t, ic), i, dma=True)
                E.fire("trl", ("f", f))
                for oc in range(8):
                    E.wait(sc, "kpe", ("t", f, oc))
                    if f >= 2:
                        E.wait(sc, "vp2", ("ft", (f - 2) * FFT + FFT - 1))
                    else:
                        E.wait(sc, "dmm", ("u", NU - 1))
                    i = sc.activation(kact[f % 2][:, oc * FFT * 128:(oc + 1) * FFT * 128],
                                      PRJ[oc % 2][:, 0:FFT * 128], AF.Relu)
                    E.fire("kre", ("e", f, oc), i)
                E.fire("kre", ("f", f))
                for t in range(FFT):
                    ft = f * FFT + t
                    if f == 0 and t == 0:
                        E.wait(sc, "sig", ("g", NCH - 1))
                    if ft >= FFT:
                        E.wait(sc, "rev", ("ft", ft - FFT))
                    last = None
                    for h2 in range(2):
                        E.wait(sc, "rpe", ("h", ft, h2))
                        last = sc.activation(fsig[f % 2][:, t * 1024 + h2 * 512:t * 1024 + (h2 + 1) * 512],
                                             PRJ[h2][:, 0:512], AF.Exp, scale=-1.0)
                    E.fire("fsg", ("ft", ft), last)
                    E.fire("fsg", ("h", ft, 0))
                    E.fire("fsg", ("h", ft, 1))
                for t in range(FFT):
                    ft = f * FFT + t
                    E.wait(sc, "f4s", ("ft", ft))
                    sc.activation(st[:, 28 + t:29 + t], st[:, 26 + t:27 + t], AF.Ln, bias=cst[:, 0:1])
                    i = sc.activation(st[:, 30 + t:31 + t], st[:, 28 + t:29 + t], AF.Exp, scale=-0.5)
                    E.fire("f4r", ("ft", ft), i)

    def emit_vector(v):
        v.memset(bsrc[0:1, :], 0.0)
        v.memset(cst[:, 0:1], EPS_LN)
        v.memset(zrow[0:1, :], 0.0)
        v.memset(mz2[:], 0.0)
        v.memset(zmast[:], 0.0)
        E.wait(v, "bp1", 0)
        v.tensor_scalar(brow[0:1, 32:48], brow[0:1, 16:32], 1.0, None, ALU.add)
        v.reciprocal(brow[0:1, 32:48], brow[0:1, 32:48])
        i = v.tensor_tensor(brow[0:1, 48:64], brow[0:1, 16:32], brow[0:1, 32:48], ALU.mult)
        E.fire("bp2", 0, i)
        E.wait(v, "bp3", 0)
        v.reciprocal(brow[0:1, 80:96], brow[0:1, 48:64])
        v.tensor_copy(bsrc[0:1, 0:16], brow[0:1, 32:48])
        v.tensor_copy(bsrc[0:1, 16:32], brow[0:1, 64:80])
        i = v.tensor_copy(bsrc[0:1, 32:48], brow[0:1, 80:96])
        E.fire("bp4", 0, i)
        E.wait(v, "bt5", "done")
        i = None
        for ii in range(2):
            i = v.tensor_copy(
                vtm[ii][:].rearrange("p (s c q e) -> p s c q e", s=C, c=8, q=2)[:, :, :, :, 64],
                bcast[:, 32:48].rearrange("p (c q) -> p c q", q=2)
                .unsqueeze(1).broadcast_to([128, C, 8, 2]))
        E.fire("bet", 0, i)


        def stats4(st, src, mean_c, var_c, tmpa, tmpb):
            v.bn_stats(st[:, 0:6], src[:, 0:512])
            v.bn_stats(st[:, 6:12], src[:, 512:1024])
            m4 = st[:, 1:12:3]
            cv4 = st[:, 2:12:3]
            v.reduce_sum(st[:, mean_c:mean_c + 1], m4, axis=AX)
            v.tensor_scalar(st[:, mean_c:mean_c + 1], st[:, mean_c:mean_c + 1], 0.25, None, ALU.mult)
            v.tensor_tensor(m4, m4, m4, ALU.mult)
            v.reduce_sum(st[:, tmpa:tmpa + 1], m4, axis=AX)
            v.tensor_scalar(st[:, tmpa:tmpa + 1], st[:, tmpa:tmpa + 1], 0.25, None, ALU.mult)
            v.reduce_sum(st[:, tmpb:tmpb + 1], cv4, axis=AX)
            v.tensor_scalar(st[:, tmpb:tmpb + 1], st[:, tmpb:tmpb + 1], 1.0 / 1024.0, None, ALU.mult)
            v.tensor_tensor(st[:, tmpa:tmpa + 1], st[:, tmpa:tmpa + 1], st[:, tmpb:tmpb + 1], ALU.add)
            v.tensor_tensor(st[:, tmpb:tmpb + 1], st[:, mean_c:mean_c + 1], st[:, mean_c:mean_c + 1], ALU.mult)
            return v.tensor_tensor(st[:, var_c:var_c + 1], st[:, tmpa:tmpa + 1], st[:, tmpb:tmpb + 1], ALU.subtract)

        def ln1_stats(gc):
            st = lnst[gc % 2]
            E.wait(v, f"xin{gc % 2}", ("chunk", gc))
            if gc >= 2:
                E.wait(v, "ln1", ("g", gc - 2))
            for sl in range(C):
                i = stats4(st, xch[gc % 2][:, sl * 1024:(sl + 1) * 1024],
                           12 + sl, 14 + sl, 22, 23)
            E.fire("ln1s", ("g", gc), i)
            E.wait(v, "ln1r", ("g", gc))
            i = v.scalar_tensor_tensor(st[:, 20:22], st[:, 12:14], -1.0, st[:, 18:20],
                                       ALU.mult, ALU.mult)
            E.fire("ln1n", ("g", gc), i)

        def sig_dve(gc):
            E.wait(v, "qke", ("g", gc))
            if gc >= 1:
                E.wait(v, "sge", ("g", gc - 1))
            if gc >= 2:
                E.wait(v, "dmm", ("u", uidx(gc - 2, C - 1, 7)))
            v.tensor_scalar(scra[gc % 2][:], qfm[gc % 2][:], 0.0, None, ALU.min)
            v.tensor_scalar(sqf[gc % 2][:], qfm[gc % 2][:], 0.0, None, ALU.max)
            v.tensor_scalar(scrb[gc % 2][:], kfm[gc % 2][:], 0.0, None, ALU.min)
            i = v.tensor_scalar(skf[gc % 2][:], kfm[gc % 2][:], 0.0, None, ALU.max)
            E.fire("sgd", ("g", gc), i)
            E.wait(v, "sge", ("g", gc))
            v.tensor_tensor(sqf[gc % 2][:], sqf[gc % 2][:], scra[gc % 2][:], ALU.add)
            i = v.tensor_tensor(skf[gc % 2][:], skf[gc % 2][:], scrb[gc % 2][:], ALU.add)
            E.fire("sig", ("g", gc), i)

        def v_evac(gc):
            for sl in range(C):
                for h2 in range(2):
                    j = 2 * sl + h2
                    E.wait(v, "qpe", ("v", gc, j))
                    if gc >= 2:
                        E.wait(v, "adm", ("u", uidx(gc - 2, C - 1, 7)))
                    i = v.tensor_copy(
                        vtm[gc % 2][:].rearrange("p (s c q e) -> p s c q e", s=C, c=8, q=2)
                        [:, sl, 4 * h2:4 * (h2 + 1), :, 0:64],
                        PRJ[h2][:, 0:512].rearrange("p (c q e) -> p c q e", q=2, e=64))
                    E.fire("vev", ("e", gc, j), i)
            E.fire("vev", ("g", gc))

        def unit_dve(gc, sl, c):
            u = uidx(gc, sl, c)
            ring = gc % 2
            rs = rsc[u % 2]
            E.wait(v, "exp", ("u", u))
            i = v.tensor_tensor(ptsb[u % 2][:].rearrange("p (q n) -> p q n", q=2),
                                ptsb[u % 2][:].rearrange("p (q n) -> p q n", q=2),
                                mask01[:].unsqueeze(1).broadcast_to([128, 2, 128]), ALU.mult)
            E.fire("msk", ("u", u), i)
            E.wait(v, "avm", ("u", u))
            if u >= 2:
                E.wait(v, "o1", ("u", u - 2))
            v.tensor_scalar(rs[:, 0:2], vprev_ap(u, 0)[:, 128:130], EPS_A, -1.0, ALU.add, ALU.mult)
            v.tensor_scalar(rs[:, 2:4], vprev_ap(u, 1)[:, 128:130], EPS_A, -1.0, ALU.add, ALU.mult)
            v.reciprocal(rs[:, 0:4], rs[:, 0:4])
            E.wait(v, "dmm", ("u", u - 2), optional=True)
            for par in range(2):
                for b in range(2):
                    i = v.scalar_tensor_tensor(
                        wsb[u % 2][64 * b:64 * (b + 1), par * 64:(par + 1) * 64],
                        vprev_ap(u, b)[64 * b:64 * (b + 1), par * 64:(par + 1) * 64],
                        rs[64 * b:64 * (b + 1), 2 * b + par:2 * b + par + 1],
                        vtm[ring][64 * b:64 * (b + 1),
                                  sl * 1040 + c * 130 + par * 65:sl * 1040 + c * 130 + par * 65 + 64],
                        ALU.mult, ALU.add)
            E.fire("wrd", ("u", u), i)
            v.tensor_scalar(rs[:, 4:6], amem_ap(u, 0)[:, 128:130], EPS_A, None, ALU.add)
            v.tensor_scalar(rs[:, 6:8], amem_ap(u, 1)[:, 128:130], EPS_A, None, ALU.add)
            v.reciprocal(rs[:, 4:8], rs[:, 4:8])
            v.tensor_tensor(rs[:, 4:8].rearrange("p (b q) -> p b q", b=2),
                            rs[:, 4:8].rearrange("p (b q) -> p b q", b=2),
                            bcast[:, 2 * c:2 * c + 2].unsqueeze(1).broadcast_to([128, 2, 2]),
                            ALU.mult)
            i = None
            for par in range(2):
                for b in range(2):
                    i = v.tensor_scalar(
                        t2sb[u % 2][64 * b:64 * (b + 1), par * 64:(par + 1) * 64],
                        amem_ap(u, b)[64 * b:64 * (b + 1), par * 64:(par + 1) * 64],
                        rs[64 * b:64 * (b + 1), 4 + 2 * b + par:5 + 2 * b + par], None, ALU.mult)
            E.fire("p1d", ("u", u), i)
            E.wait(v, "adm", ("u", u))
            v.reciprocal(rs[:, 8:9], adot_ap(u, 0)[:, 64:65])
            i = v.reciprocal(rs[:, 9:10], adot_ap(u, 1)[:, 64:65])
            E.fire("dnr", ("u", u), i)
            E.wait(v, "o1", ("u", u))
            if u >= 2:
                E.wait(v, f"tra{gc % 2}", ("g", gc - 2), optional=True)
            i = v.tensor_tensor(
                atm[ring][:, sl * 1024 + c * 128:sl * 1024 + (c + 1) * 128],
                t2sb[u % 2][:], t1sb[u % 2][:], ALU.add)
            E.fire("o2", ("u", u), i)

        def step_m(gc, sl):
            t = gc * C + sl
            ring = gc % 2
            E.wait(v, "dmm", ("u", uidx(gc, sl, 7)))
            if t >= 1:
                E.wait(v, "mbs", ("t", t - 1))
            mzv = mz2[:].rearrange("p (cb n) -> p cb n", n=130)
            for mb_i in range(2):
                for b in range(2):
                    v.tensor_copy(
                        mz2[64 * b:64 * (b + 1), :]
                        .rearrange("p (c two n) -> p c two n", two=2, n=130)[:, 4 * mb_i:4 * (mb_i + 1), b, 64 * b:64 * b + 64],
                        MBK[mb_i][64 * b:64 * (b + 1), :]
                        .rearrange("p (c q) -> p c q", q=128)[:, :, 64 * b:64 * b + 64])
            for mb_i in range(2):
                v.tensor_copy(mstg[0:64, 256 * mb_i:256 * (mb_i + 1)]
                              .rearrange("p (c n) -> p c n", n=64),
                              MBK[mb_i][0:64, :].rearrange("p (c q) -> p c q", q=128)[:, :, 64:128])
                v.tensor_copy(mstg[64:128, 256 * mb_i:256 * (mb_i + 1)]
                              .rearrange("p (c n) -> p c n", n=64),
                              MBK[mb_i][64:128, :].rearrange("p (c q) -> p c q", q=128)[:, :, 0:64])
            v.reduce_sum(zinc[:].rearrange("p (c b) -> p c b", b=2),
                         skf[ring][:].rearrange("p (ic s bj) -> p ic s bj", ic=8, s=C)
                         [:, :, sl, :].rearrange("p ic (b j) -> p ic b j", b=2), axis=AX)
            v.tensor_tensor(zmast[:], zmast[:], zinc[:], ALU.add)
            v.tensor_copy(mz2[0:64, :].rearrange("p (cb n) -> p cb n", n=130)[:, :, 128],
                          zmast[0:64, :])
            i = v.tensor_copy(mz2[64:128, :].rearrange("p (cb n) -> p cb n", n=130)[:, :, 129],
                              zmast[64:128, :])
            E.fire("mzs", ("t", t), i)

        def wo_dve(gw):
            for t in range(C):
                w = C * gw + t
                st = l2st[w % 2]
                E.wait(v, "wpe", ("w", w))
                if w >= 2:
                    E.wait(v, "aoc", ("w", w - 2))
                v.bn_stats(st[:, 0:6], PRJ[0][:, 0:512])
                v.bn_stats(st[:, 6:12], PRJ[1][:, 0:512])
                m4 = st[:, 1:12:3]
                cv4 = st[:, 2:12:3]
                v.reduce_sum(st[:, 12:13], m4, axis=AX)
                v.tensor_scalar(st[:, 12:13], st[:, 12:13], 0.25, None, ALU.mult)
                v.tensor_tensor(m4, m4, m4, ALU.mult)
                v.reduce_sum(st[:, 13:14], m4, axis=AX)
                v.tensor_scalar(st[:, 13:14], st[:, 13:14], 0.25, None, ALU.mult)
                v.reduce_sum(st[:, 14:15], cv4, axis=AX)
                v.tensor_scalar(st[:, 14:15], st[:, 14:15], 1.0 / 1024.0, None, ALU.mult)
                v.tensor_tensor(st[:, 13:14], st[:, 13:14], st[:, 14:15], ALU.add)
                v.tensor_tensor(st[:, 14:15], st[:, 12:13], st[:, 12:13], ALU.mult)
                i = v.tensor_tensor(st[:, 4:5], st[:, 13:14], st[:, 14:15], ALU.subtract)
                E.fire("l2s", ("w", w), i)
                E.wait(v, "l2r", ("w", w))
                i = v.scalar_tensor_tensor(st[:, 10:11], st[:, 12:13],
                                           -1.0, st[:, 9:10], ALU.mult, ALU.mult)
                E.fire("l2n", ("w", w), i)
                E.wait(v, "aoc", ("w", w))
                E.wait(v, f"x2o{gw % 2}", ("chunk", gw - 2), optional=True)
                i = v.tensor_tensor(
                    x2ch[gw % 2][:, t * 1024:(t + 1) * 1024],
                    aosc[w % 2][:],
                    xch[gw % 2][:, t * 1024:(t + 1) * 1024], ALU.add)
                E.fire("rsd", ("w", w), i)

        for gc in range(NCH):
            if gc >= 2:
                wo_dve(gc - 2)
            ln1_stats(gc)
            v_evac(gc)
            sig_dve(gc)
            if gc >= 1:
                for sl in range(C):
                    for c in range(8):
                        unit_dve(gc - 1, sl, c)
                    step_m(gc - 1, sl)
        for sl in range(C):
            for c in range(8):
                unit_dve(NCH - 1, sl, c)
            step_m(NCH - 1, sl)
        if NCH >= 2:
            wo_dve(NCH - 2)
        wo_dve(NCH - 1)

        if do_ffn:
            E.wait(v, "dmm", ("u", NU - 1))
            E.wait(v, f"trk{(NCH - 1) % 2}", ("g", NCH - 1))
            for f in range(NFF):
                st = f4st[f % 2]
                E.wait(v, f"x2i{f % 2}", ("f", f))
                if f >= 2:
                    E.wait(v, "ln3", ("f", f - 2))
                for t in range(FFT):
                    i = stats4(st, x2f[f % 2][:, t * 1024:(t + 1) * 1024],
                               12 + t, 14 + t, 22, 23)
                E.fire("f3s", ("f", f), i)
                E.wait(v, "f3r", ("f", f))
                i = v.scalar_tensor_tensor(st[:, 20:22], st[:, 12:14], -1.0, st[:, 18:20],
                                           ALU.mult, ALU.mult)
                E.fire("f3n", ("f", f), i)
                E.wait(v, "trl", ("f", f))
                if f >= 2:
                    E.wait(v, "kpe", ("f", f - 2))
                    E.wait(v, "rpe", ("ft", (f - 2) * FFT + FFT - 1))
                else:
                    E.wait(v, "dmm", ("u", NU - 1))
                if f % 8 == 0:
                    v.memset(x3fm_s[:].rearrange("p (ic n) -> p ic n", n=272)[:, :, 15], 0.0)
                else:
                    v.tensor_copy(
                        x3fm_s[:].rearrange("p (ic n) -> p ic n", n=272)[:, :, 15],
                        csv[:])
                for ic in range(8):
                    blk = x3fm_s[:, ic * 272:(ic + 1) * 272]
                    d_ap = scrb[f % 2][:, ic * 256:(ic + 1) * 256]
                    v.tensor_tensor(d_ap, blk[:, 16:272], blk[:, 15:271], ALU.subtract)
                    v.scalar_tensor_tensor(xrfm[f % 2][:, ic * 256:(ic + 1) * 256],
                                           d_ap, mu_fm[:, ic:ic + 1], blk[:, 15:271],
                                           ALU.mult, ALU.add)
                    i = v.scalar_tensor_tensor(xkfm[f % 2][:, ic * 256:(ic + 1) * 256],
                                               d_ap, mu_fm[:, 8 + ic:9 + ic], blk[:, 15:271],
                                               ALU.mult, ALU.add)
                i = v.tensor_copy(csv[:],
                                  x3fm_s[:].rearrange("p (ic n) -> p ic n", n=272)[:, :, 271])
                E.fire("mix", ("f", f), i)
                E.wait(v, "kre", ("f", f))
                i = v.tensor_tensor(kact[f % 2][:], kact[f % 2][:], kact[f % 2][:], ALU.mult)
                E.fire("kae", ("f", f), i)
                for t in range(FFT):
                    ft = f * FFT + t
                    sF = f4st[f % 2]
                    E.wait(v, "fsg", ("ft", ft))
                    v.tensor_scalar(rtm[f % 2][:, t * 1024:(t + 1) * 1024],
                                    fsig[f % 2][:, t * 1024:(t + 1) * 1024], 1.0, None, ALU.add)
                    with nc.allow_low_precision(reason="sigmoid output in bf16 (r in (0,1))"):
                        i = v.reciprocal(rtm[f % 2][:, t * 1024:(t + 1) * 1024],
                                         rtm[f % 2][:, t * 1024:(t + 1) * 1024])
                    E.fire("rev", ("ft", ft), i)
                    if ft < 2:
                        E.wait(v, "rsd", ("w", NWT - 2 + ft))
                    for h2 in range(2):
                        E.wait(v, "vp2", ("h", ft, h2))
                        i = v.tensor_tensor(fsc[ft % 2][:, 512 * h2:512 * (h2 + 1)],
                                            PRJ[h2][:, 0:512],
                                            rtm[f % 2][:, t * 1024 + 512 * h2:t * 1024 + 512 * (h2 + 1)],
                                            ALU.mult)
                        E.fire("ffm", ("h", ft, h2), i)
                    E.fire("ffm", ("ft", ft))
                    i = stats4(sF, fsc[ft % 2][:], 24, 26 + t, 22, 23)
                    E.fire("f4s", ("ft", ft), i)
                    E.wait(v, "f4r", ("ft", ft))
                    if f >= 2:
                        lastt = (f - 2) * FFT + FFT - 1
                        E.wait(v, f"out{((f - 2)) % 2}", ("ft", lastt))
                    else:
                        alias_g = NCH - 2 + (f % 2)
                        E.wait(v, f"x2o{alias_g % 2}", ("chunk", alias_g))
                    v.scalar_tensor_tensor(sF[:, 25:26], sF[:, 24:25], -1.0, sF[:, 30 + t:31 + t],
                                           ALU.mult, ALU.mult)
                    v.tensor_scalar(fout[f % 2][:, 1024 * t:1024 * (t + 1)],
                                    x2f[f % 2][:, t * 1024:(t + 1) * 1024], sF[:, 25:26], None, ALU.add)
                    i = v.scalar_tensor_tensor(
                        fout[f % 2][:, 1024 * t:1024 * (t + 1)],
                        fsc[ft % 2][:], sF[:, 30 + t:31 + t],
                        fout[f % 2][:, 1024 * t:1024 * (t + 1)],
                        ALU.mult, ALU.add)
                    E.fire("ffo", ("ft", ft), i)

    class _MockInstr:
        def then_inc(self, s, n):
            return self

    class _Mock:
        def __getattr__(self, k):
            return lambda *a, **kw: _MockInstr()

    mock = _Mock()
    emit_gpsimd(mock)
    emit_sync(mock)
    emit_tensor(mock)
    emit_scalar(ChainEng(mock, E, "act"))
    emit_vector(ChainEng(mock, E, "dve"))
    _count_chains = dict(E.chains)
    _count_counts = dict(E.counts)
    E.reset()

    with nc.Block() as block:
        @block.gpsimd
        def _(g):
            emit_gpsimd(g)

        @block.sync
        def _(s):
            emit_sync(s)

        @block.tensor
        def _(te):
            emit_tensor(te)

        @block.scalar
        def _(sc):
            emit_scalar(ChainEng(sc, E, "act"))

        @block.vector
        def _(v):
            emit_vector(ChainEng(v, E, "dve"))

    assert E.chains == _count_chains, ("chain mismatch", E.chains, _count_chains)
    assert E.counts == _count_counts, ("count mismatch")

    ctx.close()
    return nc, dbg


_MASK = None


def _consts():
    global _MASK
    if _MASK is None:
        m = np.zeros((128, 128), np.float32)
        for b in range(2):
            blk = np.tril(np.ones((64, 64), np.float32))
            m[64 * b:64 * (b + 1), 64 * b:64 * (b + 1)] = blk.T
        _MASK = m
    return _MASK, np.eye(128, dtype=np.float32)


def kernel(L0_attn_x, L0_attn_W_q, L0_attn_W_k, L0_attn_W_v, L0_attn_W_o,
           L0_attn_beta_param, L0_ffn_W_r, L0_ffn_W_k, L0_ffn_W_v,
           L0_ffn_mu_r, L0_ffn_mu_k):
    x = np.ascontiguousarray(np.asarray(L0_attn_x, dtype=np.float32))
    mask, iden = _consts()
    nc, _ = build_nc()
    in_maps = []
    for core in range(8):
        in_maps.append({
            "x": np.ascontiguousarray(x[2 * core:2 * core + 2]),
            "wq": np.asarray(L0_attn_W_q, np.float32),
            "wk": np.asarray(L0_attn_W_k, np.float32),
            "wv": np.asarray(L0_attn_W_v, np.float32),
            "wo": np.asarray(L0_attn_W_o, np.float32),
            "wr": np.asarray(L0_ffn_W_r, np.float32),
            "wfk": np.asarray(L0_ffn_W_k, np.float32),
            "wfv": np.asarray(L0_ffn_W_v, np.float32),
            "beta": np.asarray(L0_attn_beta_param, np.float32).reshape(1, 16),
            "mur": np.asarray(L0_ffn_mu_r, np.float32).reshape(1, D),
            "muk": np.asarray(L0_ffn_mu_k, np.float32).reshape(1, D),
            "mask01": mask,
            "iden": iden,
        })
    res = run_bass_kernel_spmd(nc, in_maps, core_ids=list(range(8)))
    return np.concatenate([r["out"] for r in res.results], axis=0)


# version 05
# speedup vs baseline: 1.0072x; 1.0072x over previous
import numpy as np
import ml_dtypes
from contextlib import ExitStack

import concourse.bass as bass
import concourse.mybir as mybir
from concourse.bass_utils import run_bass_kernel_spmd

F32 = mybir.dt.float32
BF16 = mybir.dt.bfloat16
AF = mybir.ActivationFunctionType
ALU = mybir.AluOpType
AX = mybir.AxisListType.X

B_LOC = 2
T = 2048
D = 1024
SEG = 64
S_SEG = 32
C = 2
CT = C * 128
FFT = 2
EPS_A = 1e-6
EPS_LN = 1e-5


CHAINED = {
    "ln1s": "dve", "ln1n": "dve", "sgd": "dve", "sig": "dve", "vev": "dve",
    "msk": "dve", "wrd": "dve", "dnr": "dve", "o2": "dve", "mzs": "dve",
    "l2s": "dve", "l2n": "dve", "rsd": "dve", "bp2": "dve", "bp4": "dve",
    "bet": "dve", "f3s": "dve", "f3n": "dve", "mix": "dve", "kae": "dve",
    "rev": "dve", "ffm": "dve", "f4s": "dve", "ffo": "dve", "p1d": "dve",
    "ln1r": "act", "ln1": "act", "sge": "act", "qke": "act", "exp": "act",
    "o1": "act", "l2r": "act", "aoc": "act", "wtr": "act", "f3r": "act",
    "ln3": "act", "kre": "act", "fsg": "act", "f4r": "act", "bp1": "act",
    "bp3": "act",
}


class Env:

    def __init__(self, sems):
        self.sems = sems
        self.counts = {}
        self.chains = {"dve": 0, "act": 0}
        self.events = {}
        self.mode = "count"

    def reset(self):
        self.counts = {}
        self.chains = {"dve": 0, "act": 0}
        self.mode = "emit"

    def fire(self, sem_name, event, instr=None, dma=False):
        if sem_name in CHAINED:
            ch = CHAINED[sem_name]
            if self.mode == "count":
                self.events[(sem_name, event)] = (ch, self.chains[ch])
            return
        if instr is None:
            v = self.counts.get(sem_name, 0)
            if self.mode == "count":
                self.events[(sem_name, event)] = (sem_name, v)
            return v
        inc = 16 if dma else 1
        v = self.counts.get(sem_name, 0) + inc
        self.counts[sem_name] = v
        if self.mode == "count":
            self.events[(sem_name, event)] = (sem_name, v)
        if self.mode == "emit":
            instr.then_inc(self.sems[sem_name], inc)
        return v

    def wait(self, eng, sem_name, event, optional=False):
        if self.mode == "count":
            return
        key = (sem_name, event)
        if key not in self.events:
            if optional:
                return
            raise KeyError(f"unknown event {key}")
        actual, v = self.events[key]
        if v > 0:
            eng.wait_ge(self.sems[actual], v)


class ChainEng:
    _PASS = {"wait_ge", "nop", "engine_nop", "then_inc", "dma_start"}

    def __init__(self, eng, E, key):
        self.eng = eng
        self.E = E
        self.key = key

    def __getattr__(self, name):
        raw = getattr(self.eng, name)
        if name in self._PASS or not callable(raw):
            return raw
        E = self.E
        key = self.key

        def call(*a, **kw):
            n = E.chains[key]
            if E.mode == "emit" and n > 0:
                self.eng.wait_ge(E.sems[key], n)
            instr = raw(*a, **kw)
            if E.mode == "emit":
                instr.then_inc(E.sems[key], 1)
            E.chains[key] = n + 1
            return instr

        return call


def build_nc(n_chunks=S_SEG // C, do_ffn=True, debug=0, x2_out=False):
    NCH = n_chunks
    NU = NCH * C * 8
    NST = NCH * C
    NFF = (B_LOC * T) // (FFT * 128) if do_ffn else 0
    NFT = NFF * FFT
    NWT = NCH * C

    nc = bass.Bass("TRN2", target_bir_lowering=False)

    x_in = nc.dram_tensor("x", [B_LOC, T, D], F32, kind="ExternalInput").ap()
    w_hbm = {wn: nc.dram_tensor(wn, [D, D], F32, kind="ExternalInput").ap()
             for wn in ["wq", "wk", "wv", "wo", "wr", "wfk", "wfv"]}
    beta_in = nc.dram_tensor("beta", [1, 16], F32, kind="ExternalInput").ap()
    mur_in = nc.dram_tensor("mur", [1, D], F32, kind="ExternalInput").ap()
    muk_in = nc.dram_tensor("muk", [1, D], F32, kind="ExternalInput").ap()
    mask_in = nc.dram_tensor("mask01", [128, 128], F32, kind="ExternalInput").ap()
    iden_in = nc.dram_tensor("iden", [128, 128], F32, kind="ExternalInput").ap()
    out_hbm = nc.dram_tensor("out", [B_LOC, T, D], F32, kind="ExternalOutput").ap()
    x2_hbm = nc.dram_tensor("x2scr", [B_LOC, T, D], F32,
                            kind="ExternalOutput" if x2_out else "Internal").ap()
    mb_hbm = nc.dram_tensor("mbscr", [2, 64, 512], BF16, kind="Internal").ap()

    dbg = {}
    if debug:
        for name, cols in [("d_xln1", C * 1024), ("d_qfm", 8 * CT), ("d_kfm", 8 * CT),
                           ("d_sqf", 8 * CT), ("d_skf", 8 * CT), ("d_sktm", C * 1024),
                           ("d_vtm", C * 1040), ("d_atm", C * 1024), ("d_mz", 2080),
                           ("d_x2", C * 1024), ("d_f_kact", 2048), ("d_f_rtm", 2048),
                           ("d_f_xr", 2048), ("d_f_xk", 2048), ("d_f_xln3", 2048),
                           ("d_f_fsc", 1024), ("d_f_x3fm", 2176)]:
            dbg[name] = nc.dram_tensor(name, [128, cols], F32, kind="ExternalOutput").ap()

    ctx = ExitStack()
    sb = lambda n, sh, dt_: ctx.enter_context(nc.sbuf_tensor(n, sh, dt_))
    sem_ = lambda n: ctx.enter_context(nc.semaphore(n))

    wt = {k: sb(f"wt_{k}", [128, 8 * 1024], BF16) for k in ["q", "k", "v", "o"]}
    wstage = [sb(f"wstage{i}", [128, 1024], BF16) for i in range(2)]
    iden = sb("idensb", [128, 128], BF16)
    mask01 = sb("masksb", [128, 128], BF16)
    brow = sb("brow", [1, 128], F32)
    bsrc = sb("bsrc", [1, 64], F32)
    bcast = sb("bcast", [128, 64], F32)
    zrow = sb("zrow", [1, 128], BF16)
    mu_fm = sb("mu_fm", [128, 16], F32)
    cst = sb("cst", [128, 4], F32)
    zmast = sb("zmast", [128, 16], F32)
    zinc = sb("zinc", [128, 16], F32)
    mz2 = sb("mz2", [128, 16 * 130], BF16)
    mstg = sb("mstg", [128, 512], BF16)

    xch = [sb(f"xch{i}", [128, C * 1024], F32) for i in range(2)]
    xln1 = [sb(f"xln1_{i}", [128, C * 1024], BF16) for i in range(2)]
    xfm = [sb(f"xfm{i}", [128, 8 * CT], BF16) for i in range(2)]
    qfm = [sb(f"qfm{i}", [128, 8 * CT], BF16) for i in range(2)]
    kfm = [sb(f"kfm{i}", [128, 8 * CT], BF16) for i in range(2)]
    sqf = [sb(f"sqf{i}", [128, 8 * CT], BF16) for i in range(2)]
    skf = [sb(f"skf{i}", [128, 8 * CT], BF16) for i in range(2)]
    _scra = sb("scra", [128, 8 * CT], BF16)
    _scrb = sb("scrb", [128, 8 * CT], BF16)
    scra = [_scra, _scra]
    scrb = [_scrb, _scrb]
    sktm = [sb(f"sktm{i}", [128, C * 1024], BF16) for i in range(2)]
    vtm = [sb(f"vtm{i}", [128, C * 1040], BF16) for i in range(2)]
    atm = [sb(f"atm{i}", [128, C * 1024], BF16) for i in range(2)]
    afm = [sb(f"afm{i}", [128, 8 * CT], BF16) for i in range(2)]
    x2ch = [sb(f"x2ch{i}", [128, C * 1024], F32) for i in range(2)]
    aosc = [sb(f"aosc{i}", [128, 1024], BF16) for i in range(2)]
    ptsb = [sb(f"ptsb{i}", [128, 256], BF16) for i in range(2)]
    wsb = [sb(f"wsb{i}", [128, 128], BF16) for i in range(2)]
    t1sb = [sb(f"t1sb{i}", [128, 128], BF16) for i in range(2)]
    t2sb = [sb(f"t2sb{i}", [128, 128], BF16) for i in range(2)]
    rsc = [sb(f"rsc{i}", [128, 12], F32) for i in range(2)]
    lnst = [sb(f"lnst{i}", [128, 24], F32) for i in range(2)]
    l2st = [sb(f"l2st{i}", [128, 16], F32) for i in range(2)]
    f4st = [sb(f"f4st{i}", [128, 32], F32) for i in range(2)]

    x2f, xln3, xrfm, xkfm, kact, rtm, fsig, fsc, fout = (
        xch, xln1, qfm, kfm, sqf, skf, scra, aosc, x2ch)
    x3fm_s = sb("x3fm", [128, 8 * 272], BF16)
    x3fm = [x3fm_s, x3fm_s]
    csv = sb("csv", [128, 8], BF16)
    sqscr = atm[0][:, 0:1024]

    pbank = [ctx.enter_context(nc.psum_tensor(f"pb{i}", [128, 512], F32)) for i in range(8)]
    PRJ = [pbank[0], pbank[1]]
    MBK = [pbank[2], pbank[3]]

    def _slot(u, i):
        return pbank[4 + 2 * (u % 2) + i]

    def pt_ap(u, par):
        return _slot(u, par)[:, 0:128]

    def amem_ap(u, b):
        return _slot(u, b)[:, 128:258]

    def vprev_ap(u, b):
        return _slot(u, b)[:, 258:388]

    def adot_ap(u, par):
        return _slot(u, par)[:, 388:453]

    def mtile_ap(c, par, b):
        return MBK[c // 4][64 * b:64 * (b + 1),
                           (c % 4) * 128 + par * 64:(c % 4) * 128 + par * 64 + 64]

    sem_names = ["xin0", "xin1", "wst0", "wst1", "mub", "trx0", "trx1", "trk0", "trk1",
                 "tra0", "tra1", "x2o0", "x2o1", "x2i0", "x2i1", "out0", "out1", "trl",
                 "mbl0", "mbl1",
                 "mbs", "ln1s", "ln1r", "ln1n", "ln1", "sgd", "sge", "sig",
                 "qke", "vev", "bp1", "bp2", "bp3", "bp4", "bt5", "bet",
                 "exp", "msk", "wrd", "dnr", "mzs", "o1", "o2",
                 "l2s", "l2r", "l2n", "aoc", "rsd",
                 "pmm", "avm", "adm", "dmm", "qpe", "wpe", "wtm", "wtr",
                 "f3s", "f3r", "f3n", "ln3", "mix", "kpe", "kre", "kae",
                 "rpe", "fsg", "rev", "vp2", "ffm", "f4s", "f4r", "ffo", "p1d", "dve", "act"]
    sems = {n: sem_(n) for n in sem_names}
    E = Env(sems)

    def fmcol(ic, sl, b=0, j0=0):
        return ic * CT + sl * 128 + b * 64 + j0

    def uidx(g, sl, c):
        return g * (C * 8) + sl * 8 + c

    WL_A = [("wq", wt["q"]), ("wk", wt["k"]), ("wv", wt["v"]), ("wo", wt["o"])]
    WL_F = [("wr", wt["q"]), ("wfk", wt["k"]), ("wfv", wt["v"])]
    NSA = 32


    def emit_gpsimd(g):
        i = g.dma_start(out=brow[0:1, 0:16], in_=beta_in[:, :]); E.fire("mub", "beta", i, dma=True)
        i = g.dma_start(out=mu_fm[:, 0:8], in_=mur_in[0, :].rearrange("(ic p) -> p ic", p=128),
                        allow_slow_non_contiguous=True); E.fire("mub", "mur", i, dma=True)
        i = g.dma_start(out=mu_fm[:, 8:16], in_=muk_in[0, :].rearrange("(ic p) -> p ic", p=128),
                        allow_slow_non_contiguous=True); E.fire("mub", "muk", i, dma=True)
        i = g.dma_start(out=mask01[:], in_=mask_in[:, :]); E.fire("mub", "mask", i, dma=True)
        i = g.dma_start(out=iden[:], in_=iden_in[:, :]); E.fire("mub", "iden", i, dma=True)
        E.fire("mub", "all")

        E.wait(g, "bp4", 0)
        i = g.dma_start(out=bcast[0:1, :], in_=bsrc[0:1, :]); E.fire("bt5", "d0", i, dma=True)
        r = 1
        k = 0
        while r < 128:
            E.wait(g, "bt5", f"d{k}")
            n = min(r, 128 - r)
            i = g.dma_start(out=bcast[r:r + n, :], in_=bcast[0:n, :])
            k += 1
            E.fire("bt5", f"d{k}", i, dma=True)
            r += n
        E.fire("bt5", "done")

        for wi, (wn, _) in enumerate(WL_A):
            for rr in range(8):
                idx = wi * 8 + rr
                if idx >= 2:
                    E.wait(g, "wtm", ("stage", idx - 2))
                i = g.dma_start(out=wstage[idx % 2][:],
                                in_=w_hbm[wn][128 * rr:128 * (rr + 1), :])
                E.fire(f"wst{idx % 2}", ("st", idx), i, dma=True)

        def x_load(gc):
            if gc >= 2:
                E.wait(g, "rsd", ("w", 2 * (gc - 2) + 1))
            for b in range(B_LOC):
                i = g.dma_start(
                    out=xch[gc % 2][:].rearrange("p (s n) -> p s n", n=1024)[64 * b:64 * (b + 1), :, :],
                    in_=x_in[b, 128 * gc:128 * (gc + 1), :].rearrange("(s j) d -> j s d", j=64))
                E.fire(f"xin{gc % 2}", ("x", gc, b), i, dma=True)
            E.fire(f"xin{gc % 2}", ("chunk", gc))

        def bounce(t):
            E.wait(g, "mzs", ("t", t))
            for half in range(2):
                i = g.dma_start(out=mb_hbm[half, :, :], in_=mstg[64 * half:64 * (half + 1), :])
                E.fire("mbs", ("s", t, half), i, dma=True)
            E.fire("mbs", ("t", t))
            E.wait(g, "mbs", ("t", t))
            mzv = mz2[:].rearrange("p (c two n) -> p c two n", two=2, n=130)
            i = g.dma_start(out=mzv[64:128, :, 0, 64:128],
                            in_=mb_hbm[0, :, :].rearrange("p (cc n) -> p cc n", n=64))
            E.fire(f"mbl{t % 2}", ("l", t, 0), i, dma=True)
            i = g.dma_start(out=mzv[0:64, :, 1, 0:64],
                            in_=mb_hbm[1, :, :].rearrange("p (cc n) -> p cc n", n=64))
            E.fire(f"mbl{t % 2}", ("l", t, 1), i, dma=True)
            E.fire(f"mbl{t % 2}", ("t", t))

        def x2_store(gc):
            E.wait(g, "rsd", ("w", 2 * gc + 1))
            for b in range(B_LOC):
                i = g.dma_start(
                    out=x2_hbm[b, 128 * gc:128 * (gc + 1), :].rearrange("(s j) d -> j s d", j=64),
                    in_=x2ch[gc % 2][:].rearrange("p (s n) -> p s n", n=1024)[64 * b:64 * (b + 1), :, :])
                E.fire(f"x2o{gc % 2}", ("s", gc, b), i, dma=True)
            E.fire(f"x2o{gc % 2}", ("chunk", gc))

        for gc in range(NCH):
            x_load(gc)
            if gc >= 1:
                for sl in range(C):
                    bounce((gc - 1) * C + sl)
            if gc >= 2:
                x2_store(gc - 2)
        for sl in range(C):
            bounce((NCH - 1) * C + sl)
        if NCH >= 2:
            x2_store(NCH - 2)
        x2_store(NCH - 1)

        if do_ffn:
            for wi, (wn, _) in enumerate(WL_F):
                for rr in range(8):
                    idx = NSA + wi * 8 + rr
                    E.wait(g, "wtm", ("stage", idx - 2))
                    if wi == 0 and rr == 0:
                        E.wait(g, "qpe", ("chunk", NCH - 1))
                    i = g.dma_start(out=wstage[idx % 2][:],
                                    in_=w_hbm[wn][128 * rr:128 * (rr + 1), :])
                    E.fire(f"wst{idx % 2}", ("st", idx), i, dma=True)
            def x2_load(f):
                if f >= 2:
                    E.wait(g, "ffo", ("ft", (f - 2) * FFT + FFT - 1))
                else:
                    E.wait(g, "rsd", ("w", NWT - 1))
                b = (f * FFT * 128) // T
                t0 = (f * FFT * 128) % T
                need = (t0 + FFT * 128) // 128
                E.wait(g, f"x2o{(need - 1) % 2}", ("chunk", need - 1))
                if need >= 2:
                    E.wait(g, f"x2o{(need - 2) % 2}", ("chunk", need - 2))
                alias_g = NCH - 2 + (f % 2)
                E.wait(g, f"x2o{alias_g % 2}", ("chunk", alias_g))
                i = g.dma_start(
                    out=x2f[f % 2][:].rearrange("p (t n) -> p t n", n=1024),
                    in_=x2_hbm[b, t0:t0 + FFT * 128, :].rearrange("(t p) d -> p t d", p=128))
                E.fire(f"x2i{f % 2}", ("f", f), i, dma=True)

            def out_store(ft):
                E.wait(g, "ffo", ("ft", ft))
                b = (ft * 128) // T
                t0 = (ft * 128) % T
                i = g.dma_start(out=out_hbm[b, t0:t0 + 128, :],
                                in_=fout[(ft // FFT) % 2][:, 1024 * (ft % FFT):1024 * (ft % FFT + 1)])
                E.fire(f"out{(ft // FFT) % 2}", ("ft", ft), i, dma=True)

            for f in range(NFF):
                x2_load(f)
                if f >= 1:
                    for t in range(FFT):
                        out_store((f - 1) * FFT + t)
            for t in range(FFT):
                out_store((NFF - 1) * FFT + t)

        if debug:
            if debug >= 1 and NCH >= 1:
                E.wait(g, "sig", ("g", 0))
                E.wait(g, "trk0", ("g", 0))
                E.wait(g, "vev", ("g", 0))
                E.wait(g, "ln1", ("g", 0))
                for nm, src in [("d_xln1", xln1[0]), ("d_qfm", qfm[0]), ("d_kfm", kfm[0]),
                                ("d_sqf", sqf[0]), ("d_skf", skf[0]), ("d_sktm", sktm[0]),
                                ("d_vtm", vtm[0])]:
                    i = g.dma_start(out=dbg[nm], in_=src[:])
                    E.fire("out0", ("dbg", nm), i, dma=True)
            if debug >= 2:
                E.wait(g, "o2", ("u", C * 8 - 1))
                E.wait(g, f"mbl{(C - 1) % 2}", ("t", C - 1))
                i = g.dma_start(out=dbg["d_atm"], in_=atm[0][:]); E.fire("out0", ("dbg", "d_atm"), i, dma=True)
                i = g.dma_start(out=dbg["d_mz"], in_=mz2[:]); E.fire("out0", ("dbg", "d_mz"), i, dma=True)
            if debug >= 3:
                E.wait(g, "rsd", ("w", C - 1))
                i = g.dma_start(out=dbg["d_x2"], in_=x2ch[0][:]); E.fire("out0", ("dbg", "d_x2"), i, dma=True)
            if debug >= 4 and do_ffn:
                E.wait(g, "ffo", ("ft", NFT - 1))
                for nm, src_ap in [("d_f_kact", kact[1][:]), ("d_f_rtm", rtm[1][:]),
                                   ("d_f_xr", xrfm[1][:]), ("d_f_xk", xkfm[1][:]),
                                   ("d_f_xln3", xln3[1][:]), ("d_f_fsc", fsc[1][:]),
                                   ("d_f_x3fm", x3fm_s[:])]:
                    i = g.dma_start(out=dbg[nm], in_=src_ap)
                    E.fire("out0", ("dbg", nm), i, dma=True)

        if E.mode == "emit":
            for so in ("out0", "out1", "out"):
                if E.counts.get(so, 0) > 0:
                    g.wait_ge(sems[so], E.counts[so])
        if not do_ffn:
            E.wait(g, "rsd", ("w", NWT - 1))

    def emit_sync(s):
        E.wait(s, "bt5", "done")
        for gc in range(NCH):
            E.wait(s, "ln1", ("g", gc))
            if gc >= 2:
                E.wait(s, "qpe", ("chunk", gc - 2))
            for sl in range(C):
                for ic in range(8):
                    i = s.dma_start(out=xfm[gc % 2][:, fmcol(ic, sl):fmcol(ic, sl) + 128],
                                    in_=xln1[gc % 2][:, sl * 1024 + ic * 128:sl * 1024 + (ic + 1) * 128],
                                    transpose=True)
                    E.fire(f"trx{gc % 2}", ("d", gc, sl, ic), i, dma=True)
            E.fire(f"trx{gc % 2}", ("g", gc))
            E.wait(s, "sig", ("g", gc))
            if gc >= 2:
                E.wait(s, "dmm", ("u", uidx(gc - 2, C - 1, 7)))
            for sl in range(C):
                for ic in range(8):
                    i = s.dma_start(out=sktm[gc % 2][:, sl * 1024 + ic * 128:sl * 1024 + (ic + 1) * 128],
                                    in_=skf[gc % 2][:, fmcol(ic, sl):fmcol(ic, sl) + 128],
                                    transpose=True)
                    E.fire(f"trk{gc % 2}", ("d", gc, sl, ic), i, dma=True)
            E.fire(f"trk{gc % 2}", ("g", gc))

    def emit_tensor(te):
        bank_free = {}

        def use_bank(bid, free_list):
            for (sn, ev) in bank_free.get(bid, []):
                E.wait(te, sn, ev)
            bank_free[bid] = free_list

        def w_transpose(idx):
            E.wait(te, f"wst{idx % 2}", ("st", idx))
            for ic in range(8):
                gseq = idx * 8 + ic
                use_bank(gseq % 2, [("wtr", ("e", gseq))])
                i = te.matmul(PRJ[gseq % 2][:, 0:128], wstage[idx % 2][:, 128 * ic:128 * (ic + 1)],
                              iden[:], start=True, stop=True)
                E.fire("wtm", ("mm", gseq), i)
            E.fire("wtm", ("stage", idx))

        E.wait(te, "mub", "all")
        for idx in range(NSA):
            w_transpose(idx)

        E.wait(te, "bet", 0)
        for mb_i in range(2):
            for c4 in range(4):
                te.matmul(MBK[mb_i][:, 128 * c4:128 * (c4 + 1)], zrow[0:1, :], zrow[0:1, :],
                          start=(c4 == 0), stop=(c4 == 3))

        def proj_qk(gc, wtt, dst_kind):
            E.wait(te, f"trx{gc % 2}", ("g", gc))
            for oc in range(8):
                use_bank(oc % 2, [("qke", ("e", gc, dst_kind, oc))])
                i = None
                for ic in range(8):
                    i = te.matmul(PRJ[oc % 2][:, 0:CT],
                                  wtt[:, ic * 1024 + oc * 128:ic * 1024 + oc * 128 + 128],
                                  xfm[gc % 2][:, ic * CT:(ic + 1) * CT],
                                  start=(ic == 0), stop=(ic == 7))
                E.fire("qpe", ("t", gc, dst_kind, oc), i)

        def proj_v(gc):
            for sl in range(C):
                for h2 in range(2):
                    j = 2 * sl + h2
                    use_bank(h2, [("vev", ("e", gc, j))])
                    i = None
                    for ic in range(8):
                        i = te.matmul(PRJ[h2][:, 0:512],
                                      xfm[gc % 2][:, fmcol(ic, sl):fmcol(ic, sl) + 128],
                                      wt["v"][:, ic * 1024 + h2 * 512:ic * 1024 + (h2 + 1) * 512],
                                      start=(ic == 0), stop=(ic == 7))
                    E.fire("qpe", ("v", gc, j), i)
            E.fire("qpe", ("chunk", gc))

        def wo_chunk(gc):
            E.wait(te, f"tra{gc % 2}", ("g", gc))
            for t in range(C):
                w = C * gc + t
                for h2 in range(2):
                    use_bank(h2, [("rsd", ("w", w))])
                    i = None
                    for ic in range(8):
                        i = te.matmul(PRJ[h2][:, 0:512],
                                      afm[gc % 2][:, fmcol(ic, t):fmcol(ic, t) + 128],
                                      wt["o"][:, ic * 1024 + h2 * 512:ic * 1024 + (h2 + 1) * 512],
                                      start=(ic == 0), stop=(ic == 7))
                    E.fire("wpe", ("h", w, h2), i)
                E.fire("wpe", ("w", w))
            E.fire("wpe", ("chunk", gc))

        def scan_unit(gc, sl, c):
            u = uidx(gc, sl, c)
            t = gc * C + sl
            ring = gc % 2
            E.wait(te, "sig", ("g", gc))
            if u >= 2:
                E.wait(te, "o2", ("u", u - 2))
            i = None
            for par in range(2):
                i = te.matmul(pt_ap(u, par),
                              kfm[ring][64 * par:64 * (par + 1), fmcol(c, sl):fmcol(c, sl) + 128],
                              qfm[ring][64 * par:64 * (par + 1), fmcol(c, sl):fmcol(c, sl) + 128],
                              start=True, stop=True, tile_position=(64 * par, 0))
            E.fire("pmm", ("u", u), i)
            if t >= 1:
                E.wait(te, "mzs", ("t", t - 1))
                E.wait(te, f"mbl{(t - 1) % 2}", ("t", t - 1))
            for b in range(2):
                te.matmul(amem_ap(u, b), sqf[ring][:, fmcol(c, sl):fmcol(c, sl) + 128],
                          mz2[:, (2 * c + b) * 130:(2 * c + b + 1) * 130],
                          start=True, stop=True)
            i = None
            for b in range(2):
                i = te.matmul(vprev_ap(u, b), skf[ring][:, fmcol(c, sl):fmcol(c, sl) + 128],
                              mz2[:, (2 * c + b) * 130:(2 * c + b + 1) * 130],
                              start=True, stop=True)
            E.fire("avm", ("u", u), i)
            E.wait(te, "msk", ("u", u))
            E.wait(te, "p1d", ("u", u))
            E.wait(te, "vev", ("g", gc))
            i = None
            for par in range(2):
                i = te.matmul(adot_ap(u, par), ptsb[u % 2][:, 128 * par:128 * (par + 1)],
                              vtm[ring][:, sl * 1040 + c * 130 + par * 65:sl * 1040 + c * 130 + (par + 1) * 65],
                              start=True, stop=True)
            E.fire("adm", ("u", u), i)
            E.wait(te, f"trk{gc % 2}", ("g", gc))
            if t >= 1 and c == 0:
                E.wait(te, "mzs", ("t", t - 1))
            i = None
            for par in range(2):
                for b in range(2):
                    i = te.matmul(mtile_ap(c, par, b),
                                  sktm[ring][64 * b:64 * (b + 1),
                                             sl * 1024 + c * 128 + par * 64:sl * 1024 + c * 128 + (par + 1) * 64],
                                  wsb[u % 2][64 * b:64 * (b + 1), par * 64:(par + 1) * 64],
                                  start=False, stop=False, skip_group_check=True,
                                  tile_position=(64 * b, 64 * b))
            E.fire("dmm", ("u", u), i)

        for gc in range(NCH):
            if gc >= 2:
                wo_chunk(gc - 2)
            proj_qk(gc, wt["q"], "q")
            proj_qk(gc, wt["k"], "k")
            proj_v(gc)
            if gc >= 1:
                for sl in range(C):
                    for c in range(8):
                        scan_unit(gc - 1, sl, c)
        for sl in range(C):
            for c in range(8):
                scan_unit(NCH - 1, sl, c)
        if NCH >= 2:
            wo_chunk(NCH - 2)
        wo_chunk(NCH - 1)

        if do_ffn:
            for idx in range(NSA, NSA + 24):
                w_transpose(idx)

            for f in range(NFF):
                E.wait(te, "mix", ("f", f))
                for oc in range(8):
                    use_bank(oc % 2, [("kre", ("e", f, oc))])
                    i = None
                    for ic in range(8):
                        i = te.matmul(PRJ[oc % 2][:, 0:FFT * 128],
                                      wt["k"][:, ic * 1024 + oc * 128:ic * 1024 + oc * 128 + 128],
                                      xkfm[f % 2][:, ic * FFT * 128:(ic + 1) * FFT * 128],
                                      start=(ic == 0), stop=(ic == 7))
                    E.fire("kpe", ("t", f, oc), i)
                E.fire("kpe", ("f", f))
                for t in range(FFT):
                    ft = f * FFT + t
                    for h2 in range(2):
                        use_bank(h2, [("fsg", ("h", ft, h2))])
                        i = None
                        for ic in range(8):
                            i = te.matmul(PRJ[h2][:, 0:512],
                                          xrfm[f % 2][:, ic * FFT * 128 + t * 128:ic * FFT * 128 + (t + 1) * 128],
                                          wt["q"][:, ic * 1024 + h2 * 512:ic * 1024 + (h2 + 1) * 512],
                                          start=(ic == 0), stop=(ic == 7))
                        E.fire("rpe", ("h", ft, h2), i)
                    E.fire("rpe", ("ft", ft))
                E.wait(te, "kae", ("f", f))
                for t in range(FFT):
                    ft = f * FFT + t
                    for h2 in range(2):
                        use_bank(h2, [("ffm", ("h", ft, h2))])
                        i = None
                        for ic in range(8):
                            i = te.matmul(PRJ[h2][:, 0:512],
                                          kact[f % 2][:, ic * FFT * 128 + t * 128:ic * FFT * 128 + (t + 1) * 128],
                                          wt["v"][:, ic * 1024 + h2 * 512:ic * 1024 + (h2 + 1) * 512],
                                          start=(ic == 0), stop=(ic == 7))
                        E.fire("vp2", ("h", ft, h2), i)
                    E.fire("vp2", ("ft", ft))

    def emit_scalar(sc):
        E.wait(sc, "mub", "all")
        i = sc.activation(brow[0:1, 16:32], brow[0:1, 0:16], AF.Exp, scale=-1.0)
        E.fire("bp1", 0, i)
        E.wait(sc, "bp2", 0)
        i = sc.activation(brow[0:1, 64:80], brow[0:1, 48:64], AF.Ln)
        E.fire("bp3", 0, i)

        for idx in range(NSA):
            wname, dst = WL_A[idx // 8]
            rr = idx % 8
            for ic in range(8):
                gseq = idx * 8 + ic
                E.wait(sc, "wtm", ("mm", gseq))
                i = sc.copy(dst[:, ic * 1024 + 128 * rr:ic * 1024 + 128 * (rr + 1)],
                            PRJ[gseq % 2][:, 0:128])
                E.fire("wtr", ("e", gseq), i)

        def ln1_act(gc):
            st = lnst[gc % 2]
            E.wait(sc, "ln1s", ("g", gc))
            sc.activation(st[:, 16:18], st[:, 14:16], AF.Ln, bias=cst[:, 0:1])
            i = sc.activation(st[:, 18:20], st[:, 16:18], AF.Exp, scale=-0.5)
            E.fire("ln1r", ("g", gc), i)
            E.wait(sc, "ln1n", ("g", gc))
            if gc >= 2:
                E.wait(sc, f"trx{gc % 2}", ("g", gc - 2))
            i = None
            for sl in range(C):
                i = sc.activation(xln1[gc % 2][:, sl * 1024:(sl + 1) * 1024],
                                  xch[gc % 2][:, sl * 1024:(sl + 1) * 1024], AF.Identity,
                                  bias=st[:, 20 + sl:21 + sl], scale=st[:, 18 + sl:19 + sl])
            E.fire("ln1", ("g", gc), i)

        def qk_evac(gc):
            for kind, dstb in [("q", qfm), ("k", kfm)]:
                for oc in range(8):
                    E.wait(sc, "qpe", ("t", gc, kind, oc))
                    if gc >= 2:
                        E.wait(sc, "dmm", ("u", uidx(gc - 2, C - 1, 7)))
                    i = sc.copy(dstb[gc % 2][:, oc * CT:(oc + 1) * CT], PRJ[oc % 2][:, 0:CT])
                    E.fire("qke", ("e", gc, kind, oc), i)
            E.fire("qke", ("g", gc))

        def sig_exp(gc):
            E.wait(sc, "sgd", ("g", gc))
            sc.activation(scra[gc % 2][:], scra[gc % 2][:], AF.Exp)
            i = sc.activation(scrb[gc % 2][:], scrb[gc % 2][:], AF.Exp)
            E.fire("sge", ("g", gc), i)

        def unit_act(gc, sl, c):
            u = uidx(gc, sl, c)
            E.wait(sc, "avm", ("u", u))
            if u >= 2:
                E.wait(sc, "adm", ("u", u - 2))
            i = None
            for par in range(2):
                i = sc.activation(ptsb[u % 2][:, 128 * par:128 * (par + 1)],
                                  pt_ap(u, par), AF.Exp, scale=0.125,
                                  bias=bcast[:, 16 + 2 * c + par:17 + 2 * c + par])
            E.fire("exp", ("u", u), i)
            E.wait(sc, "dnr", ("u", u))
            if u >= 2:
                E.wait(sc, "o2", ("u", u - 2))
            i = None
            for par in range(2):
                i = sc.activation(t1sb[u % 2][:, 64 * par:64 * (par + 1)],
                                  adot_ap(u, par)[:, 0:64], AF.Identity,
                                  scale=rsc[u % 2][:, 8 + par:9 + par])
            E.fire("o1", ("u", u), i)

        def tra_issue(gp):
            E.wait(sc, "o2", ("u", uidx(gp, C - 1, 7)))
            if gp >= 2:
                E.wait(sc, "wpe", ("chunk", gp - 2))
            for sl in range(C):
                for ic in range(8):
                    i = sc.dma_start(out=afm[gp % 2][:, fmcol(ic, sl):fmcol(ic, sl) + 128],
                                     in_=atm[gp % 2][:, sl * 1024 + ic * 128:sl * 1024 + (ic + 1) * 128],
                                     transpose=True)
                    E.fire(f"tra{gp % 2}", ("d", gp, sl, ic), i, dma=True)
            E.fire(f"tra{gp % 2}", ("g", gp))

        def wo_act(gw):
            for t in range(C):
                w = C * gw + t
                st = l2st[w % 2]
                E.wait(sc, "l2s", ("w", w))
                sc.activation(st[:, 8:9], st[:, 4:5], AF.Ln, bias=cst[:, 0:1])
                i = sc.activation(st[:, 9:10], st[:, 8:9], AF.Exp, scale=-0.5)
                E.fire("l2r", ("w", w), i)
                E.wait(sc, "l2n", ("w", w))
                if w >= 2:
                    E.wait(sc, "rsd", ("w", w - 2))
                i = None
                for h2 in range(2):
                    i = sc.activation(aosc[w % 2][:, 512 * h2:512 * (h2 + 1)],
                                      PRJ[h2][:, 0:512], AF.Identity,
                                      bias=st[:, 10:11], scale=st[:, 9:10])
                E.fire("aoc", ("w", w), i)

        for gc in range(NCH):
            if gc >= 2:
                wo_act(gc - 2)
            ln1_act(gc)
            qk_evac(gc)
            sig_exp(gc)
            if gc >= 1:
                for sl in range(C):
                    for c in range(8):
                        unit_act(gc - 1, sl, c)
                tra_issue(gc - 1)
        for sl in range(C):
            for c in range(8):
                unit_act(NCH - 1, sl, c)
        tra_issue(NCH - 1)
        if NCH >= 2:
            wo_act(NCH - 2)
        wo_act(NCH - 1)

        if do_ffn:
            for idx in range(NSA, NSA + 24):
                wname, dst = WL_F[(idx - NSA) // 8]
                rr = idx % 8
                for ic in range(8):
                    gseq = idx * 8 + ic
                    E.wait(sc, "wtm", ("mm", gseq))
                    i = sc.copy(dst[:, ic * 1024 + 128 * rr:ic * 1024 + 128 * (rr + 1)],
                                PRJ[gseq % 2][:, 0:128])
                    E.fire("wtr", ("e", gseq), i)
            for f in range(NFF):
                st = f4st[f % 2]
                E.wait(sc, "f3s", ("f", f))
                sc.activation(st[:, 16:18], st[:, 14:16], AF.Ln, bias=cst[:, 0:1])
                i = sc.activation(st[:, 18:20], st[:, 16:18], AF.Exp, scale=-0.5)
                E.fire("f3r", ("f", f), i)
                E.wait(sc, "f3n", ("f", f))
                if f >= 2:
                    E.wait(sc, "trl", ("f", f - 2))
                else:
                    E.wait(sc, f"trx{(NCH - 2 + (f % 2)) % 2}", ("g", NCH - 2 + (f % 2)))
                i = None
                for t in range(FFT):
                    i = sc.activation(xln3[f % 2][:, t * 1024:(t + 1) * 1024],
                                      x2f[f % 2][:, t * 1024:(t + 1) * 1024], AF.Identity,
                                      bias=st[:, 20 + t:21 + t], scale=st[:, 18 + t:19 + t])
                E.fire("ln3", ("f", f), i)
                E.wait(sc, "ln3", ("f", f))
                if f >= 1:
                    E.wait(sc, "mix", ("f", f - 1))
                for t in range(FFT):
                    for ic in range(8):
                        i = sc.dma_start(
                            out=x3fm_s[:, ic * 272 + 16 + t * 128:ic * 272 + 16 + (t + 1) * 128],
                            in_=xln3[f % 2][:, t * 1024 + ic * 128:t * 1024 + (ic + 1) * 128],
                            transpose=True)
                        E.fire("trl", ("d", f, t, ic), i, dma=True)
                E.fire("trl", ("f", f))
                for oc in range(8):
                    E.wait(sc, "kpe", ("t", f, oc))
                    if f >= 2:
                        E.wait(sc, "vp2", ("ft", (f - 2) * FFT + FFT - 1))
                    else:
                        E.wait(sc, "dmm", ("u", NU - 1))
                    i = sc.activation(kact[f % 2][:, oc * FFT * 128:(oc + 1) * FFT * 128],
                                      PRJ[oc % 2][:, 0:FFT * 128], AF.Relu)
                    E.fire("kre", ("e", f, oc), i)
                E.fire("kre", ("f", f))
                for t in range(FFT):
                    ft = f * FFT + t
                    if f == 0 and t == 0:
                        E.wait(sc, "sig", ("g", NCH - 1))
                    if ft >= FFT:
                        E.wait(sc, "rev", ("ft", ft - FFT))
                    last = None
                    for h2 in range(2):
                        E.wait(sc, "rpe", ("h", ft, h2))
                        last = sc.activation(fsig[f % 2][:, t * 1024 + h2 * 512:t * 1024 + (h2 + 1) * 512],
                                             PRJ[h2][:, 0:512], AF.Exp, scale=-1.0)
                    E.fire("fsg", ("ft", ft), last)
                    E.fire("fsg", ("h", ft, 0))
                    E.fire("fsg", ("h", ft, 1))
                for t in range(FFT):
                    ft = f * FFT + t
                    E.wait(sc, "f4s", ("ft", ft))
                    sc.activation(st[:, 28 + t:29 + t], st[:, 26 + t:27 + t], AF.Ln, bias=cst[:, 0:1])
                    i = sc.activation(st[:, 30 + t:31 + t], st[:, 28 + t:29 + t], AF.Exp, scale=-0.5)
                    E.fire("f4r", ("ft", ft), i)

    def emit_vector(v):
        v.memset(bsrc[0:1, :], 0.0)
        v.memset(cst[:, 0:1], EPS_LN)
        v.memset(zrow[0:1, :], 0.0)
        v.memset(mz2[:], 0.0)
        v.memset(zmast[:], 0.0)
        E.wait(v, "bp1", 0)
        v.tensor_scalar(brow[0:1, 32:48], brow[0:1, 16:32], 1.0, None, ALU.add)
        v.reciprocal(brow[0:1, 32:48], brow[0:1, 32:48])
        i = v.tensor_tensor(brow[0:1, 48:64], brow[0:1, 16:32], brow[0:1, 32:48], ALU.mult)
        E.fire("bp2", 0, i)
        E.wait(v, "bp3", 0)
        v.reciprocal(brow[0:1, 80:96], brow[0:1, 48:64])
        v.tensor_copy(bsrc[0:1, 0:16], brow[0:1, 32:48])
        v.tensor_copy(bsrc[0:1, 16:32], brow[0:1, 64:80])
        i = v.tensor_copy(bsrc[0:1, 32:48], brow[0:1, 80:96])
        E.fire("bp4", 0, i)
        E.wait(v, "bt5", "done")
        i = None
        for ii in range(2):
            i = v.tensor_copy(
                vtm[ii][:].rearrange("p (s c q e) -> p s c q e", s=C, c=8, q=2)[:, :, :, :, 64],
                bcast[:, 32:48].rearrange("p (c q) -> p c q", q=2)
                .unsqueeze(1).broadcast_to([128, C, 8, 2]))
        E.fire("bet", 0, i)


        def stats4(st, src, mean_c, var_c, tmpa, tmpb):
            v.bn_stats(st[:, 0:6], src[:, 0:512])
            v.bn_stats(st[:, 6:12], src[:, 512:1024])
            m4 = st[:, 1:12:3]
            cv4 = st[:, 2:12:3]
            v.reduce_sum(st[:, mean_c:mean_c + 1], m4, axis=AX)
            v.tensor_scalar(st[:, mean_c:mean_c + 1], st[:, mean_c:mean_c + 1], 0.25, None, ALU.mult)
            v.tensor_tensor(m4, m4, m4, ALU.mult)
            v.reduce_sum(st[:, tmpa:tmpa + 1], m4, axis=AX)
            v.tensor_scalar(st[:, tmpa:tmpa + 1], st[:, tmpa:tmpa + 1], 0.25, None, ALU.mult)
            v.reduce_sum(st[:, tmpb:tmpb + 1], cv4, axis=AX)
            v.tensor_scalar(st[:, tmpb:tmpb + 1], st[:, tmpb:tmpb + 1], 1.0 / 1024.0, None, ALU.mult)
            v.tensor_tensor(st[:, tmpa:tmpa + 1], st[:, tmpa:tmpa + 1], st[:, tmpb:tmpb + 1], ALU.add)
            v.tensor_tensor(st[:, tmpb:tmpb + 1], st[:, mean_c:mean_c + 1], st[:, mean_c:mean_c + 1], ALU.mult)
            return v.tensor_tensor(st[:, var_c:var_c + 1], st[:, tmpa:tmpa + 1], st[:, tmpb:tmpb + 1], ALU.subtract)

        def ln1_stats(gc):
            st = lnst[gc % 2]
            E.wait(v, f"xin{gc % 2}", ("chunk", gc))
            if gc >= 2:
                E.wait(v, "ln1", ("g", gc - 2))
            for sl in range(C):
                src = xch[gc % 2][:, sl * 1024:(sl + 1) * 1024]
                v.reduce_sum(st[:, 12 + sl:13 + sl], src, axis=AX)
                v.tensor_tensor(sqscr, src, src, ALU.mult)
                v.reduce_sum(st[:, 22 + sl:23 + sl], sqscr, axis=AX)
            v.tensor_scalar(st[:, 12:14], st[:, 12:14], 1.0 / 1024.0, None, ALU.mult)
            v.tensor_scalar(st[:, 22:24], st[:, 22:24], 1.0 / 1024.0, None, ALU.mult)
            v.tensor_tensor(st[:, 16:18], st[:, 12:14], st[:, 12:14], ALU.mult)
            i = v.tensor_tensor(st[:, 14:16], st[:, 22:24], st[:, 16:18], ALU.subtract)
            E.fire("ln1s", ("g", gc), i)
            E.wait(v, "ln1r", ("g", gc))
            i = v.scalar_tensor_tensor(st[:, 20:22], st[:, 12:14], -1.0, st[:, 18:20],
                                       ALU.mult, ALU.mult)
            E.fire("ln1n", ("g", gc), i)

        def sig_dve(gc):
            E.wait(v, "qke", ("g", gc))
            if gc >= 1:
                E.wait(v, "sge", ("g", gc - 1))
            if gc >= 2:
                E.wait(v, "dmm", ("u", uidx(gc - 2, C - 1, 7)))
            v.tensor_scalar(scra[gc % 2][:], qfm[gc % 2][:], 0.0, None, ALU.min)
            v.tensor_scalar(sqf[gc % 2][:], qfm[gc % 2][:], 0.0, None, ALU.max)
            v.tensor_scalar(scrb[gc % 2][:], kfm[gc % 2][:], 0.0, None, ALU.min)
            i = v.tensor_scalar(skf[gc % 2][:], kfm[gc % 2][:], 0.0, None, ALU.max)
            E.fire("sgd", ("g", gc), i)
            E.wait(v, "sge", ("g", gc))
            v.tensor_tensor(sqf[gc % 2][:], sqf[gc % 2][:], scra[gc % 2][:], ALU.add)
            i = v.tensor_tensor(skf[gc % 2][:], skf[gc % 2][:], scrb[gc % 2][:], ALU.add)
            E.fire("sig", ("g", gc), i)

        def v_evac(gc):
            for sl in range(C):
                for h2 in range(2):
                    j = 2 * sl + h2
                    E.wait(v, "qpe", ("v", gc, j))
                    if gc >= 2:
                        E.wait(v, "adm", ("u", uidx(gc - 2, C - 1, 7)))
                    i = v.tensor_copy(
                        vtm[gc % 2][:].rearrange("p (s c q e) -> p s c q e", s=C, c=8, q=2)
                        [:, sl, 4 * h2:4 * (h2 + 1), :, 0:64],
                        PRJ[h2][:, 0:512].rearrange("p (c q e) -> p c q e", q=2, e=64))
                    E.fire("vev", ("e", gc, j), i)
            E.fire("vev", ("g", gc))

        def unit_dve(gc, sl, c):
            u = uidx(gc, sl, c)
            ring = gc % 2
            rs = rsc[u % 2]
            E.wait(v, "exp", ("u", u))
            i = v.tensor_tensor(ptsb[u % 2][:].rearrange("p (q n) -> p q n", q=2),
                                ptsb[u % 2][:].rearrange("p (q n) -> p q n", q=2),
                                mask01[:].unsqueeze(1).broadcast_to([128, 2, 128]), ALU.mult)
            E.fire("msk", ("u", u), i)
            E.wait(v, "avm", ("u", u))
            if u >= 2:
                E.wait(v, "o1", ("u", u - 2))
            v.tensor_scalar(rs[:, 0:2], vprev_ap(u, 0)[:, 128:130], EPS_A, -1.0, ALU.add, ALU.mult)
            v.tensor_scalar(rs[:, 2:4], vprev_ap(u, 1)[:, 128:130], EPS_A, -1.0, ALU.add, ALU.mult)
            v.reciprocal(rs[:, 0:4], rs[:, 0:4])
            E.wait(v, "dmm", ("u", u - 2), optional=True)
            for par in range(2):
                for b in range(2):
                    i = v.scalar_tensor_tensor(
                        wsb[u % 2][64 * b:64 * (b + 1), par * 64:(par + 1) * 64],
                        vprev_ap(u, b)[64 * b:64 * (b + 1), par * 64:(par + 1) * 64],
                        rs[64 * b:64 * (b + 1), 2 * b + par:2 * b + par + 1],
                        vtm[ring][64 * b:64 * (b + 1),
                                  sl * 1040 + c * 130 + par * 65:sl * 1040 + c * 130 + par * 65 + 64],
                        ALU.mult, ALU.add)
            E.fire("wrd", ("u", u), i)
            v.tensor_scalar(rs[:, 4:6], amem_ap(u, 0)[:, 128:130], EPS_A, None, ALU.add)
            v.tensor_scalar(rs[:, 6:8], amem_ap(u, 1)[:, 128:130], EPS_A, None, ALU.add)
            v.reciprocal(rs[:, 4:8], rs[:, 4:8])
            v.tensor_tensor(rs[:, 4:8].rearrange("p (b q) -> p b q", b=2),
                            rs[:, 4:8].rearrange("p (b q) -> p b q", b=2),
                            bcast[:, 2 * c:2 * c + 2].unsqueeze(1).broadcast_to([128, 2, 2]),
                            ALU.mult)
            i = None
            for par in range(2):
                for b in range(2):
                    i = v.tensor_scalar(
                        t2sb[u % 2][64 * b:64 * (b + 1), par * 64:(par + 1) * 64],
                        amem_ap(u, b)[64 * b:64 * (b + 1), par * 64:(par + 1) * 64],
                        rs[64 * b:64 * (b + 1), 4 + 2 * b + par:5 + 2 * b + par], None, ALU.mult)
            E.fire("p1d", ("u", u), i)
            E.wait(v, "adm", ("u", u))
            v.reciprocal(rs[:, 8:9], adot_ap(u, 0)[:, 64:65])
            i = v.reciprocal(rs[:, 9:10], adot_ap(u, 1)[:, 64:65])
            E.fire("dnr", ("u", u), i)
            E.wait(v, "o1", ("u", u))
            if u >= 2:
                E.wait(v, f"tra{gc % 2}", ("g", gc - 2), optional=True)
            i = v.tensor_tensor(
                atm[ring][:, sl * 1024 + c * 128:sl * 1024 + (c + 1) * 128],
                t2sb[u % 2][:], t1sb[u % 2][:], ALU.add)
            E.fire("o2", ("u", u), i)

        def step_m(gc, sl):
            t = gc * C + sl
            ring = gc % 2
            E.wait(v, "dmm", ("u", uidx(gc, sl, 7)))
            if t >= 1:
                E.wait(v, "mbs", ("t", t - 1))
            mzv = mz2[:].rearrange("p (cb n) -> p cb n", n=130)
            for mb_i in range(2):
                for b in range(2):
                    v.tensor_copy(
                        mz2[64 * b:64 * (b + 1), :]
                        .rearrange("p (c two n) -> p c two n", two=2, n=130)[:, 4 * mb_i:4 * (mb_i + 1), b, 64 * b:64 * b + 64],
                        MBK[mb_i][64 * b:64 * (b + 1), :]
                        .rearrange("p (c q) -> p c q", q=128)[:, :, 64 * b:64 * b + 64])
            for mb_i in range(2):
                v.tensor_copy(mstg[0:64, 256 * mb_i:256 * (mb_i + 1)]
                              .rearrange("p (c n) -> p c n", n=64),
                              MBK[mb_i][0:64, :].rearrange("p (c q) -> p c q", q=128)[:, :, 64:128])
                v.tensor_copy(mstg[64:128, 256 * mb_i:256 * (mb_i + 1)]
                              .rearrange("p (c n) -> p c n", n=64),
                              MBK[mb_i][64:128, :].rearrange("p (c q) -> p c q", q=128)[:, :, 0:64])
            v.reduce_sum(zinc[:].rearrange("p (c b) -> p c b", b=2),
                         skf[ring][:].rearrange("p (ic s bj) -> p ic s bj", ic=8, s=C)
                         [:, :, sl, :].rearrange("p ic (b j) -> p ic b j", b=2), axis=AX)
            v.tensor_tensor(zmast[:], zmast[:], zinc[:], ALU.add)
            v.tensor_copy(mz2[0:64, :].rearrange("p (cb n) -> p cb n", n=130)[:, :, 128],
                          zmast[0:64, :])
            i = v.tensor_copy(mz2[64:128, :].rearrange("p (cb n) -> p cb n", n=130)[:, :, 129],
                              zmast[64:128, :])
            E.fire("mzs", ("t", t), i)

        def wo_dve(gw):
            for t in range(C):
                w = C * gw + t
                st = l2st[w % 2]
                E.wait(v, "wpe", ("w", w))
                if w >= 2:
                    E.wait(v, "rsd", ("w", w - 2))
                v.bn_stats(st[:, 0:6], PRJ[0][:, 0:512])
                v.bn_stats(st[:, 6:12], PRJ[1][:, 0:512])
                m4 = st[:, 1:12:3]
                cv4 = st[:, 2:12:3]
                v.reduce_sum(st[:, 12:13], m4, axis=AX)
                v.tensor_scalar(st[:, 12:13], st[:, 12:13], 0.25, None, ALU.mult)
                v.tensor_tensor(m4, m4, m4, ALU.mult)
                v.reduce_sum(st[:, 13:14], m4, axis=AX)
                v.tensor_scalar(st[:, 13:14], st[:, 13:14], 0.25, None, ALU.mult)
                v.reduce_sum(st[:, 14:15], cv4, axis=AX)
                v.tensor_scalar(st[:, 14:15], st[:, 14:15], 1.0 / 1024.0, None, ALU.mult)
                v.tensor_tensor(st[:, 13:14], st[:, 13:14], st[:, 14:15], ALU.add)
                v.tensor_tensor(st[:, 14:15], st[:, 12:13], st[:, 12:13], ALU.mult)
                i = v.tensor_tensor(st[:, 4:5], st[:, 13:14], st[:, 14:15], ALU.subtract)
                E.fire("l2s", ("w", w), i)
                E.wait(v, "l2r", ("w", w))
                i = v.scalar_tensor_tensor(st[:, 10:11], st[:, 12:13],
                                           -1.0, st[:, 9:10], ALU.mult, ALU.mult)
                E.fire("l2n", ("w", w), i)
                E.wait(v, "aoc", ("w", w))
                E.wait(v, f"x2o{gw % 2}", ("chunk", gw - 2), optional=True)
                i = v.tensor_tensor(
                    x2ch[gw % 2][:, t * 1024:(t + 1) * 1024],
                    aosc[w % 2][:],
                    xch[gw % 2][:, t * 1024:(t + 1) * 1024], ALU.add)
                E.fire("rsd", ("w", w), i)

        for gc in range(NCH):
            if gc >= 2:
                wo_dve(gc - 2)
            ln1_stats(gc)
            v_evac(gc)
            sig_dve(gc)
            if gc >= 1:
                for sl in range(C):
                    for c in range(8):
                        unit_dve(gc - 1, sl, c)
                    step_m(gc - 1, sl)
        for sl in range(C):
            for c in range(8):
                unit_dve(NCH - 1, sl, c)
            step_m(NCH - 1, sl)
        if NCH >= 2:
            wo_dve(NCH - 2)
        wo_dve(NCH - 1)

        if do_ffn:
            E.wait(v, "dmm", ("u", NU - 1))
            E.wait(v, f"trk{(NCH - 1) % 2}", ("g", NCH - 1))
            for f in range(NFF):
                st = f4st[f % 2]
                E.wait(v, f"x2i{f % 2}", ("f", f))
                if f >= 2:
                    E.wait(v, "ln3", ("f", f - 2))
                for t in range(FFT):
                    src = x2f[f % 2][:, t * 1024:(t + 1) * 1024]
                    v.reduce_sum(st[:, 12 + t:13 + t], src, axis=AX)
                    v.tensor_tensor(sqscr, src, src, ALU.mult)
                    v.reduce_sum(st[:, 22 + t:23 + t], sqscr, axis=AX)
                v.tensor_scalar(st[:, 12:14], st[:, 12:14], 1.0 / 1024.0, None, ALU.mult)
                v.tensor_scalar(st[:, 22:24], st[:, 22:24], 1.0 / 1024.0, None, ALU.mult)
                v.tensor_tensor(st[:, 24:26], st[:, 12:14], st[:, 12:14], ALU.mult)
                i = v.tensor_tensor(st[:, 14:16], st[:, 22:24], st[:, 24:26], ALU.subtract)
                E.fire("f3s", ("f", f), i)
                E.wait(v, "f3r", ("f", f))
                i = v.scalar_tensor_tensor(st[:, 20:22], st[:, 12:14], -1.0, st[:, 18:20],
                                           ALU.mult, ALU.mult)
                E.fire("f3n", ("f", f), i)
                E.wait(v, "trl", ("f", f))
                if f >= 2:
                    E.wait(v, "kpe", ("f", f - 2))
                    E.wait(v, "rpe", ("ft", (f - 2) * FFT + FFT - 1))
                else:
                    E.wait(v, "dmm", ("u", NU - 1))
                x3v = x3fm_s[:].rearrange("p (ic n) -> p ic n", n=272)
                if f % 8 == 0:
                    v.memset(x3v[:, :, 15], 0.0)
                else:
                    v.tensor_copy(x3v[:, :, 15], csv[:])
                dv = scrb[f % 2][:].rearrange("p (ic n) -> p ic n", n=256)
                xrv = xrfm[f % 2][:].rearrange("p (ic n) -> p ic n", n=256)
                xkv = xkfm[f % 2][:].rearrange("p (ic n) -> p ic n", n=256)
                xprev = x3v[:, :, 15:271]
                v.tensor_tensor(dv, x3v[:, :, 16:272], xprev, ALU.subtract)
                v.tensor_tensor(xrv, dv,
                                mu_fm[:, 0:8].unsqueeze(2).broadcast_to([128, 8, 256]),
                                ALU.mult)
                v.tensor_tensor(xrv, xrv, xprev, ALU.add)
                v.tensor_tensor(xkv, dv,
                                mu_fm[:, 8:16].unsqueeze(2).broadcast_to([128, 8, 256]),
                                ALU.mult)
                v.tensor_tensor(xkv, xkv, xprev, ALU.add)
                i = v.tensor_copy(csv[:], x3v[:, :, 271])
                E.fire("mix", ("f", f), i)
                E.wait(v, "kre", ("f", f))
                i = v.tensor_tensor(kact[f % 2][:], kact[f % 2][:], kact[f % 2][:], ALU.mult)
                E.fire("kae", ("f", f), i)
                for t in range(FFT):
                    ft = f * FFT + t
                    sF = f4st[f % 2]
                    E.wait(v, "fsg", ("ft", ft))
                    v.tensor_scalar(rtm[f % 2][:, t * 1024:(t + 1) * 1024],
                                    fsig[f % 2][:, t * 1024:(t + 1) * 1024], 1.0, None, ALU.add)
                    with nc.allow_low_precision(reason="sigmoid output in bf16 (r in (0,1))"):
                        i = v.reciprocal(rtm[f % 2][:, t * 1024:(t + 1) * 1024],
                                         rtm[f % 2][:, t * 1024:(t + 1) * 1024])
                    E.fire("rev", ("ft", ft), i)
                    if ft < 2:
                        E.wait(v, "rsd", ("w", NWT - 2 + ft))
                    for h2 in range(2):
                        E.wait(v, "vp2", ("h", ft, h2))
                        i = v.tensor_tensor(
                            fsc[ft % 2][:, 512 * h2:512 * (h2 + 1)],
                            PRJ[h2][:, 0:512],
                            rtm[f % 2][:, t * 1024 + 512 * h2:t * 1024 + 512 * (h2 + 1)],
                            ALU.mult)
                        E.fire("ffm", ("h", ft, h2), i)
                    E.fire("ffm", ("ft", ft))
                    v.reduce_sum(sF[:, 10:11], fsc[ft % 2][:], axis=AX)
                    v.tensor_scalar(sF[:, 24:25], sF[:, 10:11], 1.0 / 1024.0, None, ALU.mult)
                    v.tensor_tensor(sqscr, fsc[ft % 2][:], fsc[ft % 2][:], ALU.mult)
                    v.reduce_sum(sF[:, 11:12], sqscr, axis=AX)
                    v.tensor_scalar(sF[:, 11:12], sF[:, 11:12], 1.0 / 1024.0, None, ALU.mult)
                    v.tensor_tensor(sF[:, 10:11], sF[:, 24:25], sF[:, 24:25], ALU.mult)
                    i = v.tensor_tensor(sF[:, 26 + t:27 + t], sF[:, 11:12], sF[:, 10:11],
                                        ALU.subtract)
                    E.fire("f4s", ("ft", ft), i)
                    E.wait(v, "f4r", ("ft", ft))
                    if f >= 2:
                        lastt = (f - 2) * FFT + FFT - 1
                        E.wait(v, f"out{((f - 2)) % 2}", ("ft", lastt))
                    else:
                        alias_g = NCH - 2 + (f % 2)
                        E.wait(v, f"x2o{alias_g % 2}", ("chunk", alias_g))
                    v.scalar_tensor_tensor(sF[:, 25:26], sF[:, 24:25], -1.0, sF[:, 30 + t:31 + t],
                                           ALU.mult, ALU.mult)
                    v.tensor_scalar(fout[f % 2][:, 1024 * t:1024 * (t + 1)],
                                    x2f[f % 2][:, t * 1024:(t + 1) * 1024], sF[:, 25:26], None, ALU.add)
                    i = v.scalar_tensor_tensor(
                        fout[f % 2][:, 1024 * t:1024 * (t + 1)],
                        fsc[ft % 2][:], sF[:, 30 + t:31 + t],
                        fout[f % 2][:, 1024 * t:1024 * (t + 1)],
                        ALU.mult, ALU.add)
                    E.fire("ffo", ("ft", ft), i)

    class _MockInstr:
        def then_inc(self, s, n):
            return self

    class _Mock:
        def __getattr__(self, k):
            return lambda *a, **kw: _MockInstr()

    mock = _Mock()
    emit_gpsimd(mock)
    emit_sync(mock)
    emit_tensor(mock)
    emit_scalar(ChainEng(mock, E, "act"))
    emit_vector(ChainEng(mock, E, "dve"))
    _count_chains = dict(E.chains)
    _count_counts = dict(E.counts)
    E.reset()

    with nc.Block() as block:
        @block.gpsimd
        def _(g):
            emit_gpsimd(g)

        @block.sync
        def _(s):
            emit_sync(s)

        @block.tensor
        def _(te):
            emit_tensor(te)

        @block.scalar
        def _(sc):
            emit_scalar(ChainEng(sc, E, "act"))

        @block.vector
        def _(v):
            emit_vector(ChainEng(v, E, "dve"))

    assert E.chains == _count_chains, ("chain mismatch", E.chains, _count_chains)
    assert E.counts == _count_counts, ("count mismatch")

    ctx.close()
    return nc, dbg


_MASK = None


def _consts():
    global _MASK
    if _MASK is None:
        m = np.zeros((128, 128), np.float32)
        for b in range(2):
            blk = np.tril(np.ones((64, 64), np.float32))
            m[64 * b:64 * (b + 1), 64 * b:64 * (b + 1)] = blk.T
        _MASK = m
    return _MASK, np.eye(128, dtype=np.float32)


def kernel(L0_attn_x, L0_attn_W_q, L0_attn_W_k, L0_attn_W_v, L0_attn_W_o,
           L0_attn_beta_param, L0_ffn_W_r, L0_ffn_W_k, L0_ffn_W_v,
           L0_ffn_mu_r, L0_ffn_mu_k):
    x = np.ascontiguousarray(np.asarray(L0_attn_x, dtype=np.float32))
    mask, iden = _consts()
    nc, _ = build_nc()
    in_maps = []
    for core in range(8):
        in_maps.append({
            "x": np.ascontiguousarray(x[2 * core:2 * core + 2]),
            "wq": np.asarray(L0_attn_W_q, np.float32),
            "wk": np.asarray(L0_attn_W_k, np.float32),
            "wv": np.asarray(L0_attn_W_v, np.float32),
            "wo": np.asarray(L0_attn_W_o, np.float32),
            "wr": np.asarray(L0_ffn_W_r, np.float32),
            "wfk": np.asarray(L0_ffn_W_k, np.float32),
            "wfv": np.asarray(L0_ffn_W_v, np.float32),
            "beta": np.asarray(L0_attn_beta_param, np.float32).reshape(1, 16),
            "mur": np.asarray(L0_ffn_mu_r, np.float32).reshape(1, D),
            "muk": np.asarray(L0_ffn_mu_k, np.float32).reshape(1, D),
            "mask01": mask,
            "iden": iden,
        })
    res = run_bass_kernel_spmd(nc, in_maps, core_ids=list(range(8)))
    return np.concatenate([r["out"] for r in res.results], axis=0)


# version 07
# speedup vs baseline: 1.0285x; 1.0212x over previous
import numpy as np
import ml_dtypes
from contextlib import ExitStack

import concourse.bass as bass
import concourse.mybir as mybir
from concourse.bass_utils import run_bass_kernel_spmd

F32 = mybir.dt.float32
BF16 = mybir.dt.bfloat16
AF = mybir.ActivationFunctionType
ALU = mybir.AluOpType
AX = mybir.AxisListType.X

B_LOC = 2
T = 2048
D = 1024
SEG = 64
S_SEG = 32
C = 2
CT = C * 128
FFT = 2
EPS_A = 1e-6
EPS_LN = 1e-5


CHAINED = {
    "ln1s": "dve", "ln1n": "dve", "sig": "dve", "vev": "dve",
    "msk": "dve", "dnr": "dve", "o2": "dve", "mzs": "dve",
    "l2s": "dve", "l2n": "dve", "rsd": "dve", "bp2": "dve", "bp4": "dve",
    "bet": "dve", "f3s": "dve", "f3n": "dve", "mix": "dve", "kae": "dve",
    "rev": "dve", "ffm": "dve", "f4s": "dve", "ffo": "dve", "p1d": "dve",
    "mze": "dve", "gev": "dve",
    "ln1r": "act", "ln1": "act", "sge": "act", "qke": "act", "exp": "act",
    "o1": "act", "l2r": "act", "aoc": "act", "wtr": "act", "f3r": "act",
    "ln3": "act", "kre": "act", "fsg": "act", "f4r": "act", "bp1": "act",
    "bp3": "act",
}


class Env:

    def __init__(self, sems):
        self.sems = sems
        self.counts = {}
        self.chains = {"dve": 0, "act": 0}
        self.events = {}
        self.mode = "count"

    def reset(self):
        self.counts = {}
        self.chains = {"dve": 0, "act": 0}
        self.mode = "emit"

    def fire(self, sem_name, event, instr=None, dma=False):
        if sem_name in CHAINED:
            ch = CHAINED[sem_name]
            if self.mode == "count":
                self.events[(sem_name, event)] = (ch, self.chains[ch])
            return
        if instr is None:
            v = self.counts.get(sem_name, 0)
            if self.mode == "count":
                self.events[(sem_name, event)] = (sem_name, v)
            return v
        inc = 16 if dma else 1
        v = self.counts.get(sem_name, 0) + inc
        self.counts[sem_name] = v
        if self.mode == "count":
            self.events[(sem_name, event)] = (sem_name, v)
        if self.mode == "emit":
            instr.then_inc(self.sems[sem_name], inc)
        return v

    def wait(self, eng, sem_name, event, optional=False):
        if self.mode == "count":
            return
        key = (sem_name, event)
        if key not in self.events:
            if optional:
                return
            raise KeyError(f"unknown event {key}")
        actual, v = self.events[key]
        if v > 0:
            eng.wait_ge(self.sems[actual], v)


class ChainEng:
    _PASS = {"wait_ge", "nop", "engine_nop", "then_inc", "dma_start"}

    def __init__(self, eng, E, key):
        self.eng = eng
        self.E = E
        self.key = key

    def __getattr__(self, name):
        raw = getattr(self.eng, name)
        if name in self._PASS or not callable(raw):
            return raw
        E = self.E
        key = self.key

        def call(*a, **kw):
            n = E.chains[key]
            if E.mode == "emit" and n > 0:
                self.eng.wait_ge(E.sems[key], n)
            instr = raw(*a, **kw)
            if E.mode == "emit":
                instr.then_inc(E.sems[key], 1)
            E.chains[key] = n + 1
            return instr

        return call


def build_nc(n_chunks=S_SEG // C, do_ffn=True, debug=0, x2_out=False, ablate=()):
    NCH = n_chunks
    NU = NCH * C * 8
    NST = NCH * C
    NFF = (B_LOC * T) // (FFT * 128) if do_ffn else 0
    NFT = NFF * FFT
    NWT = NCH * C

    nc = bass.Bass("TRN2", target_bir_lowering=False)

    x_in = nc.dram_tensor("x", [B_LOC, T, D], F32, kind="ExternalInput").ap()
    w_hbm = {wn: nc.dram_tensor(wn, [D, D], F32, kind="ExternalInput").ap()
             for wn in ["wq", "wk", "wv", "wo", "wr", "wfk", "wfv"]}
    beta_in = nc.dram_tensor("beta", [1, 16], F32, kind="ExternalInput").ap()
    mur_in = nc.dram_tensor("mur", [1, D], F32, kind="ExternalInput").ap()
    muk_in = nc.dram_tensor("muk", [1, D], F32, kind="ExternalInput").ap()
    mask_in = nc.dram_tensor("mask01", [128, 128], F32, kind="ExternalInput").ap()
    iden_in = nc.dram_tensor("iden", [128, 128], F32, kind="ExternalInput").ap()
    out_hbm = nc.dram_tensor("out", [B_LOC, T, D], F32, kind="ExternalOutput").ap()
    x2_hbm = nc.dram_tensor("x2scr", [B_LOC, T, D], F32,
                            kind="ExternalOutput" if x2_out else "Internal").ap()
    mb_hbm = nc.dram_tensor("mbscr", [2, 64, 512], BF16, kind="Internal").ap()

    dbg = {}
    if debug:
        for name, cols in [("d_xln1", C * 1024), ("d_qfm", 8 * CT), ("d_kfm", 8 * CT),
                           ("d_sqf", 8 * CT), ("d_skf", 8 * CT), ("d_sktm", C * 1024),
                           ("d_vtm", C * 1040), ("d_atm", C * 1024), ("d_mz", 2080),
                           ("d_x2", C * 1024), ("d_f_kact", 2048), ("d_f_rtm", 2048),
                           ("d_f_xr", 2048), ("d_f_xk", 2048), ("d_f_xln3", 2048),
                           ("d_f_fsc", 1024), ("d_f_x3fm", 2176)]:
            dbg[name] = nc.dram_tensor(name, [128, cols], F32, kind="ExternalOutput").ap()

    ctx = ExitStack()
    sb = lambda n, sh, dt_: ctx.enter_context(nc.sbuf_tensor(n, sh, dt_))
    sem_ = lambda n: ctx.enter_context(nc.semaphore(n))

    wt = {k: sb(f"wt_{k}", [128, 8 * 1024], BF16) for k in ["q", "k", "v", "o"]}
    wstage = [sb(f"wstage{i}", [128, 1024], BF16) for i in range(2)]
    iden = sb("idensb", [128, 128], BF16)
    mask01 = sb("masksb", [128, 128], BF16)
    brow = sb("brow", [1, 128], F32)
    bsrc = sb("bsrc", [1, 64], F32)
    bcast = sb("bcast", [128, 64], F32)
    zrow = sb("zrow", [1, 128], BF16)
    mu_fm = sb("mu_fm", [128, 16], F32)
    cst = sb("cst", [128, 4], F32)
    zmast = sb("zmast", [128, 16], F32)
    zinc = sb("zinc", [128, 16], F32)
    mz2 = sb("mz2", [128, 16 * 130], BF16)
    zbd = sb("zbd", [128, 2 * 32], BF16)

    xch = [sb(f"xch{i}", [128, C * 1024], F32) for i in range(2)]
    xln1 = [sb(f"xln1_{i}", [128, C * 1024], BF16) for i in range(2)]
    xfm = [sb(f"xfm{i}", [128, 8 * CT], BF16) for i in range(2)]
    qfm = [sb(f"qfm{i}", [128, 8 * CT], BF16) for i in range(2)]
    kfm = [sb(f"kfm{i}", [128, 8 * CT], BF16) for i in range(2)]
    sqf = [sb(f"sqf{i}", [128, 8 * CT], BF16) for i in range(2)]
    skf = [sb(f"skf{i}", [128, 8 * CT], BF16) for i in range(2)]
    _scra = sb("scra", [128, 8 * CT], BF16)
    _scrb = sb("scrb", [128, 8 * CT], BF16)
    scra = [_scra, _scra]
    scrb = [_scrb, _scrb]
    mzsb = _scra
    gsb = _scrb
    sktm = [sb(f"sktm{i}", [128, C * 1024], BF16) for i in range(2)]
    vtm = [sb(f"vtm{i}", [128, C * 1040], BF16) for i in range(2)]
    atm = [sb(f"atm{i}", [128, C * 1024], BF16) for i in range(2)]
    afm = [sb(f"afm{i}", [128, 8 * CT], BF16) for i in range(2)]
    x2ch = [sb(f"x2ch{i}", [128, C * 1024], F32) for i in range(2)]
    aosc = [sb(f"aosc{i}", [128, 1024], BF16) for i in range(2)]
    ptsb = [sb(f"ptsb{i}", [128, 256], BF16) for i in range(2)]
    sks = [[sb(f"sks{i}b{b}", [128, 128], BF16) for b in range(2)] for i in range(2)]
    t1sb = [sb(f"t1sb{i}", [128, 128], BF16) for i in range(2)]
    t2sb = [sb(f"t2sb{i}", [128, 128], BF16) for i in range(2)]
    rsc = [sb(f"rsc{i}", [128, 12], F32) for i in range(2)]
    lnst = [sb(f"lnst{i}", [128, 24], F32) for i in range(2)]
    l2st = [sb(f"l2st{i}", [128, 16], F32) for i in range(2)]
    f4st = [sb(f"f4st{i}", [128, 32], F32) for i in range(2)]

    x2f, xln3, xrfm, xkfm, kact, rtm, fsig, fsc, fout = (
        xch, xln1, qfm, kfm, sqf, skf, scra, aosc, x2ch)
    x3fm_s = sb("x3fm", [128, 8 * 272], BF16)
    x3fm = [x3fm_s, x3fm_s]
    csv = sb("csv", [128, 8], BF16)
    sqscr = atm[0][:, 0:1024]

    pbank = [ctx.enter_context(nc.psum_tensor(f"pb{i}", [128, 512], F32)) for i in range(8)]
    PRJ = [pbank[0], pbank[1]]
    MBK = [pbank[2], pbank[3]]

    def _slot(u, i):
        return pbank[4 + 2 * (u % 2) + i]

    def pt_ap(u, par):
        return _slot(u, par)[:, 0:128]

    def amem_ap(u, b):
        return _slot(u, b)[:, 128:258]

    def adot_ap(u, par):
        return _slot(u, par)[:, 258:323]

    def gcon_ap(u, b, par):
        return _slot(u, b)[64 * b:64 * (b + 1), 323 + 64 * par:387 + 64 * par]

    def denk_ap(u):
        return _slot(u, 0)[:, 455:459]

    def mtile_ap(c, par, b):
        return MBK[c // 4][64 * b:64 * (b + 1),
                           (c % 4) * 128 + par * 64:(c % 4) * 128 + par * 64 + 64]

    sem_names = ["xin0", "xin1", "wst0", "wst1", "mub", "trx0", "trx1", "trk0", "trk1",
                 "tra0", "tra1", "x2o0", "x2o1", "x2i0", "x2i1", "out0", "out1", "trl",
                 "mbl0", "mbl1",
                 "mbs", "ln1s", "ln1r", "ln1n", "ln1", "sge", "sig",
                 "qke", "vev", "bp1", "bp2", "bp3", "bp4", "bt5", "bet",
                 "exp", "msk", "dnr", "mzs", "o1", "o2", "gum", "mze", "gev",
                 "l2s", "l2r", "l2n", "aoc", "rsd",
                 "pmm", "avm", "adm", "dmm", "qpe", "wpe", "wtm", "wtr",
                 "f3s", "f3r", "f3n", "ln3", "mix", "kpe", "kre", "kae",
                 "rpe", "fsg", "rev", "vp2", "ffm", "f4s", "f4r", "ffo", "p1d", "dve", "act"]
    sems = {n: sem_(n) for n in sem_names}
    E = Env(sems)

    def fmcol(ic, sl, b=0, j0=0):
        return ic * CT + sl * 128 + b * 64 + j0

    def uidx(g, sl, c):
        return g * (C * 8) + sl * 8 + c

    WL_A = [("wq", wt["q"]), ("wk", wt["k"]), ("wv", wt["v"]), ("wo", wt["o"])]
    WL_F = [("wr", wt["q"]), ("wfk", wt["k"]), ("wfv", wt["v"])]
    NSA = 32


    def emit_gpsimd(g):
        i = g.dma_start(out=brow[0:1, 0:16], in_=beta_in[:, :]); E.fire("mub", "beta", i, dma=True)
        i = g.dma_start(out=mu_fm[:, 0:8], in_=mur_in[0, :].rearrange("(ic p) -> p ic", p=128),
                        allow_slow_non_contiguous=True); E.fire("mub", "mur", i, dma=True)
        i = g.dma_start(out=mu_fm[:, 8:16], in_=muk_in[0, :].rearrange("(ic p) -> p ic", p=128),
                        allow_slow_non_contiguous=True); E.fire("mub", "muk", i, dma=True)
        i = g.dma_start(out=mask01[:], in_=mask_in[:, :]); E.fire("mub", "mask", i, dma=True)
        i = g.dma_start(out=iden[:], in_=iden_in[:, :]); E.fire("mub", "iden", i, dma=True)
        E.fire("mub", "all")

        E.wait(g, "bp4", 0)
        i = g.dma_start(out=bcast[0:1, :], in_=bsrc[0:1, :]); E.fire("bt5", "d0", i, dma=True)
        r = 1
        k = 0
        while r < 128:
            E.wait(g, "bt5", f"d{k}")
            n = min(r, 128 - r)
            i = g.dma_start(out=bcast[r:r + n, :], in_=bcast[0:n, :])
            k += 1
            E.fire("bt5", f"d{k}", i, dma=True)
            r += n
        E.fire("bt5", "done")

        for wi, (wn, _) in enumerate(WL_A):
            for rr in range(8):
                idx = wi * 8 + rr
                if idx >= 2:
                    E.wait(g, "wtm", ("stage", idx - 2))
                i = g.dma_start(out=wstage[idx % 2][:],
                                in_=w_hbm[wn][128 * rr:128 * (rr + 1), :])
                E.fire(f"wst{idx % 2}", ("st", idx), i, dma=True)

        def x_load(gc):
            if gc >= 2:
                E.wait(g, "rsd", ("w", 2 * (gc - 2) + 1))
            for b in range(B_LOC):
                i = g.dma_start(
                    out=xch[gc % 2][:].rearrange("p (s n) -> p s n", n=1024)[64 * b:64 * (b + 1), :, :],
                    in_=x_in[b, 128 * gc:128 * (gc + 1), :].rearrange("(s j) d -> j s d", j=64))
                E.fire(f"xin{gc % 2}", ("x", gc, b), i, dma=True)
            E.fire(f"xin{gc % 2}", ("chunk", gc))

        def bounce(t):
            if t == 0:
                return
            E.wait(g, "mze", ("t", t))
            ring = (t % 2) * 1024
            mzq = mzsb[:, ring:ring + 1024].rearrange("p (c q) -> p c q", q=128)
            i = g.dma_start(out=mb_hbm[0, :, :].rearrange("p (cc n) -> p cc n", n=64),
                            in_=mzq[0:64, :, 64:128])
            E.fire("mbs", ("s", t, 0), i, dma=True)
            i = g.dma_start(out=mb_hbm[1, :, :].rearrange("p (cc n) -> p cc n", n=64),
                            in_=mzq[64:128, :, 0:64])
            E.fire("mbs", ("s", t, 1), i, dma=True)
            E.fire("mbs", ("t", t))
            E.wait(g, "mbs", ("t", t))
            mzv = mz2[:].rearrange("p (c two n) -> p c two n", two=2, n=130)
            i = g.dma_start(out=mzv[64:128, :, 0, 64:128],
                            in_=mb_hbm[0, :, :].rearrange("p (cc n) -> p cc n", n=64))
            E.fire(f"mbl{t % 2}", ("l", t, 0), i, dma=True)
            i = g.dma_start(out=mzv[0:64, :, 1, 0:64],
                            in_=mb_hbm[1, :, :].rearrange("p (cc n) -> p cc n", n=64))
            E.fire(f"mbl{t % 2}", ("l", t, 1), i, dma=True)
            E.fire(f"mbl{t % 2}", ("t", t))

        def x2_store(gc):
            E.wait(g, "rsd", ("w", 2 * gc + 1))
            for b in range(B_LOC):
                i = g.dma_start(
                    out=x2_hbm[b, 128 * gc:128 * (gc + 1), :].rearrange("(s j) d -> j s d", j=64),
                    in_=x2ch[gc % 2][:].rearrange("p (s n) -> p s n", n=1024)[64 * b:64 * (b + 1), :, :])
                E.fire(f"x2o{gc % 2}", ("s", gc, b), i, dma=True)
            E.fire(f"x2o{gc % 2}", ("chunk", gc))

        for gc in range(NCH):
            x_load(gc)
            if gc >= 1:
                for sl in range(C):
                    bounce((gc - 1) * C + sl)
            if gc >= 2:
                x2_store(gc - 2)
        for sl in range(C):
            bounce((NCH - 1) * C + sl)
        if NCH >= 2:
            x2_store(NCH - 2)
        x2_store(NCH - 1)

        if do_ffn:
            for wi, (wn, _) in enumerate(WL_F):
                for rr in range(8):
                    idx = NSA + wi * 8 + rr
                    E.wait(g, "wtm", ("stage", idx - 2))
                    if wi == 0 and rr == 0:
                        E.wait(g, "qpe", ("chunk", NCH - 1))
                    i = g.dma_start(out=wstage[idx % 2][:],
                                    in_=w_hbm[wn][128 * rr:128 * (rr + 1), :])
                    E.fire(f"wst{idx % 2}", ("st", idx), i, dma=True)
            def x2_load(f):
                if f >= 2:
                    E.wait(g, "ffo", ("ft", (f - 2) * FFT + FFT - 1))
                else:
                    E.wait(g, "rsd", ("w", NWT - 1))
                b = (f * FFT * 128) // T
                t0 = (f * FFT * 128) % T
                need = (t0 + FFT * 128) // 128
                E.wait(g, f"x2o{(need - 1) % 2}", ("chunk", need - 1))
                if need >= 2:
                    E.wait(g, f"x2o{(need - 2) % 2}", ("chunk", need - 2))
                alias_g = NCH - 2 + (f % 2)
                E.wait(g, f"x2o{alias_g % 2}", ("chunk", alias_g))
                i = g.dma_start(
                    out=x2f[f % 2][:].rearrange("p (t n) -> p t n", n=1024),
                    in_=x2_hbm[b, t0:t0 + FFT * 128, :].rearrange("(t p) d -> p t d", p=128))
                E.fire(f"x2i{f % 2}", ("f", f), i, dma=True)

            def out_store(ft):
                E.wait(g, "ffo", ("ft", ft))
                b = (ft * 128) // T
                t0 = (ft * 128) % T
                i = g.dma_start(out=out_hbm[b, t0:t0 + 128, :],
                                in_=fout[(ft // FFT) % 2][:, 1024 * (ft % FFT):1024 * (ft % FFT + 1)])
                E.fire(f"out{(ft // FFT) % 2}", ("ft", ft), i, dma=True)

            for f in range(NFF):
                x2_load(f)
                if f >= 1:
                    for t in range(FFT):
                        out_store((f - 1) * FFT + t)
            for t in range(FFT):
                out_store((NFF - 1) * FFT + t)

        if debug:
            if debug >= 1 and NCH >= 1:
                E.wait(g, "sig", ("g", 0))
                E.wait(g, "trk0", ("g", 0))
                E.wait(g, "vev", ("g", 0))
                E.wait(g, "ln1", ("g", 0))
                for nm, src in [("d_xln1", xln1[0]), ("d_qfm", qfm[0]), ("d_kfm", kfm[0]),
                                ("d_sqf", sqf[0]), ("d_skf", skf[0]), ("d_sktm", sktm[0]),
                                ("d_vtm", vtm[0])]:
                    i = g.dma_start(out=dbg[nm], in_=src[:])
                    E.fire("out0", ("dbg", nm), i, dma=True)
            if debug >= 2:
                E.wait(g, "o2", ("u", C * 8 - 1))
                E.wait(g, f"mbl{(C - 1) % 2}", ("t", C - 1))
                i = g.dma_start(out=dbg["d_atm"], in_=atm[0][:]); E.fire("out0", ("dbg", "d_atm"), i, dma=True)
                i = g.dma_start(out=dbg["d_mz"], in_=mz2[:]); E.fire("out0", ("dbg", "d_mz"), i, dma=True)
            if debug >= 3:
                E.wait(g, "rsd", ("w", C - 1))
                i = g.dma_start(out=dbg["d_x2"], in_=x2ch[0][:]); E.fire("out0", ("dbg", "d_x2"), i, dma=True)
            if debug >= 4 and do_ffn:
                E.wait(g, "ffo", ("ft", NFT - 1))
                for nm, src_ap in [("d_f_kact", kact[1][:]), ("d_f_rtm", rtm[1][:]),
                                   ("d_f_xr", xrfm[1][:]), ("d_f_xk", xkfm[1][:]),
                                   ("d_f_xln3", xln3[1][:]), ("d_f_fsc", fsc[1][:]),
                                   ("d_f_x3fm", x3fm_s[:])]:
                    i = g.dma_start(out=dbg[nm], in_=src_ap)
                    E.fire("out0", ("dbg", nm), i, dma=True)

        if E.mode == "emit":
            for so in ("out0", "out1", "out"):
                if E.counts.get(so, 0) > 0:
                    g.wait_ge(sems[so], E.counts[so])
        if not do_ffn:
            E.wait(g, "rsd", ("w", NWT - 1))

    def emit_sync(s):
        E.wait(s, "bt5", "done")
        for gc in range(NCH):
            E.wait(s, "ln1", ("g", gc))
            if gc >= 2:
                E.wait(s, "qpe", ("chunk", gc - 2))
            for sl in range(C):
                for ic in range(8):
                    i = s.dma_start(out=xfm[gc % 2][:, fmcol(ic, sl):fmcol(ic, sl) + 128],
                                    in_=xln1[gc % 2][:, sl * 1024 + ic * 128:sl * 1024 + (ic + 1) * 128],
                                    transpose=True)
                    E.fire(f"trx{gc % 2}", ("d", gc, sl, ic), i, dma=True)
            E.fire(f"trx{gc % 2}", ("g", gc))
            E.wait(s, "sig", ("g", gc))
            if gc >= 2:
                E.wait(s, "dmm", ("u", uidx(gc - 2, C - 1, 7)))
            for sl in range(C):
                for ic in range(8):
                    i = s.dma_start(out=sktm[gc % 2][:, sl * 1024 + ic * 128:sl * 1024 + (ic + 1) * 128],
                                    in_=skf[gc % 2][:, fmcol(ic, sl):fmcol(ic, sl) + 128],
                                    transpose=True)
                    E.fire(f"trk{gc % 2}", ("d", gc, sl, ic), i, dma=True)
            E.fire(f"trk{gc % 2}", ("g", gc))

    def emit_tensor(te):
        bank_free = {}

        def use_bank(bid, free_list):
            for (sn, ev) in bank_free.get(bid, []):
                E.wait(te, sn, ev)
            bank_free[bid] = free_list

        def w_transpose(idx):
            E.wait(te, f"wst{idx % 2}", ("st", idx))
            for ic in range(8):
                gseq = idx * 8 + ic
                use_bank(gseq % 2, [("wtr", ("e", gseq))])
                i = te.matmul(PRJ[gseq % 2][:, 0:128], wstage[idx % 2][:, 128 * ic:128 * (ic + 1)],
                              iden[:], start=True, stop=True)
                E.fire("wtm", ("mm", gseq), i)
            E.fire("wtm", ("stage", idx))

        E.wait(te, "mub", "all")
        for idx in range(NSA):
            w_transpose(idx)

        E.wait(te, "bet", 0)
        for mb_i in range(2):
            for c4 in range(4):
                te.matmul(MBK[mb_i][:, 128 * c4:128 * (c4 + 1)], zrow[0:1, :], zrow[0:1, :],
                          start=(c4 == 0), stop=(c4 == 3))

        def proj_qk(gc, wtt, dst_kind):
            E.wait(te, f"trx{gc % 2}", ("g", gc))
            for oc in range(8):
                use_bank(oc % 2, [("qke", ("e", gc, dst_kind, oc))])
                i = None
                for ic in range(8):
                    i = te.matmul(PRJ[oc % 2][:, 0:CT],
                                  wtt[:, ic * 1024 + oc * 128:ic * 1024 + oc * 128 + 128],
                                  xfm[gc % 2][:, ic * CT:(ic + 1) * CT],
                                  start=(ic == 0), stop=(ic == 7))
                E.fire("qpe", ("t", gc, dst_kind, oc), i)

        def proj_v(gc):
            for sl in range(C):
                for h2 in range(2):
                    j = 2 * sl + h2
                    use_bank(h2, [("vev", ("e", gc, j))])
                    i = None
                    for ic in range(8):
                        i = te.matmul(PRJ[h2][:, 0:512],
                                      xfm[gc % 2][:, fmcol(ic, sl):fmcol(ic, sl) + 128],
                                      wt["v"][:, ic * 1024 + h2 * 512:ic * 1024 + (h2 + 1) * 512],
                                      start=(ic == 0), stop=(ic == 7))
                    E.fire("qpe", ("v", gc, j), i)
            E.fire("qpe", ("chunk", gc))

        def wo_chunk(gc):
            E.wait(te, f"tra{gc % 2}", ("g", gc))
            for t in range(C):
                w = C * gc + t
                for h2 in range(2):
                    use_bank(h2, [("rsd", ("w", w))])
                    i = None
                    for ic in range(8):
                        i = te.matmul(PRJ[h2][:, 0:512],
                                      afm[gc % 2][:, fmcol(ic, t):fmcol(ic, t) + 128],
                                      wt["o"][:, ic * 1024 + h2 * 512:ic * 1024 + (h2 + 1) * 512],
                                      start=(ic == 0), stop=(ic == 7))
                    E.fire("wpe", ("h", w, h2), i)
                E.fire("wpe", ("w", w))
            E.fire("wpe", ("chunk", gc))

        def scan_unit(gc, sl, c):
            u = uidx(gc, sl, c)
            t = gc * C + sl
            ring = gc % 2
            last = (t == NST - 1)
            E.wait(te, "sig", ("g", gc))
            if u >= 2:
                E.wait(te, "o2", ("u", u - 2))
            if t >= 1 and c == 0:
                E.wait(te, "mzs", ("t", t))
                E.wait(te, f"mbl{t % 2}", ("t", t))
            for par in range(2):
                te.matmul(pt_ap(u, par),
                          kfm[ring][64 * par:64 * (par + 1), fmcol(c, sl):fmcol(c, sl) + 128],
                          qfm[ring][64 * par:64 * (par + 1), fmcol(c, sl):fmcol(c, sl) + 128],
                          start=True, stop=True, tile_position=(64 * par, 0))
            i = None
            for b in range(2):
                i = te.matmul(amem_ap(u, b), sqf[ring][:, fmcol(c, sl):fmcol(c, sl) + 128],
                              mz2[:, (2 * c + b) * 130:(2 * c + b + 1) * 130],
                              start=True, stop=True)
            if t >= 1 and not last and "denk" not in ablate:
                i = te.matmul(denk_ap(u), skf[ring][:, fmcol(c, sl):fmcol(c, sl) + 128],
                              zbd[:, (t % 2) * 32 + 4 * c:(t % 2) * 32 + 4 * (c + 1)],
                              start=True, stop=True)
            E.fire("avm", ("u", u), i)
            E.wait(te, "msk", ("u", u))
            E.wait(te, "p1d", ("u", u))
            E.wait(te, "vev", ("g", gc))
            E.wait(te, f"trk{gc % 2}", ("g", gc))
            i = None
            for par in range(2):
                i = te.matmul(adot_ap(u, par), ptsb[u % 2][:, 128 * par:128 * (par + 1)],
                              vtm[ring][:, sl * 1040 + c * 130 + par * 65:sl * 1040 + c * 130 + (par + 1) * 65],
                              start=True, stop=True)
            if t >= 1 and not last and "gcon" not in ablate and "gmm" not in ablate:
                for b in range(2):
                    for par in range(2):
                        i = te.matmul(
                            gcon_ap(u, b, par),
                            sks[u % 2][b][:, 64 * par:64 * (par + 1)],
                            sktm[ring][:, sl * 1024 + c * 128 + par * 64:sl * 1024 + c * 128 + (par + 1) * 64],
                            start=True, stop=True)
            E.fire("adm", ("u", u), i)
            if not last and "u" in ablate:
                E.fire("dmm", ("u", u))
            elif not last:
                if t >= 1:
                    E.wait(te, "mze", ("t", t))
                for par in range(2):
                    for b in range(2):
                        i = te.matmul(
                            mtile_ap(c, par, b),
                            sktm[ring][64 * b:64 * (b + 1),
                                       sl * 1024 + c * 128 + par * 64:sl * 1024 + c * 128 + (par + 1) * 64],
                            vtm[ring][64 * b:64 * (b + 1),
                                      sl * 1040 + c * 130 + par * 65:sl * 1040 + c * 130 + par * 65 + 64],
                            start=False, stop=False, skip_group_check=True,
                            tile_position=(64 * b, 64 * b))
                E.fire("dmm", ("u", u), i)

        def m_chain(t):
            if t == 0 or t == NST - 1:
                return
            if "grec" in ablate:
                for half in range(2):
                    E.fire("gum", ("t", t, half))
                return
            gc, sl = t // C, t % C
            E.wait(te, "gev", ("u", uidx(gc, sl, 7)))
            ring2 = (t % 2) * 1024
            for half in range(2):
                E.wait(te, "mze", ("t", t, half))
                i = None
                for c in range(4 * half, 4 * half + 4):
                    for par in range(2):
                        for b in range(2):
                            i = te.matmul(
                                mtile_ap(c, par, b),
                                gsb[64 * b:64 * (b + 1),
                                    ring2 + c * 128 + par * 64:ring2 + c * 128 + (par + 1) * 64],
                                mzsb[64 * b:64 * (b + 1),
                                     ring2 + c * 128 + par * 64:ring2 + c * 128 + (par + 1) * 64],
                                start=False, stop=False, skip_group_check=True,
                                tile_position=(64 * b, 64 * b))
                E.fire("gum", ("t", t, half), i)

        for gc in range(NCH):
            if gc >= 2:
                wo_chunk(gc - 2)
            proj_qk(gc, wt["q"], "q")
            proj_qk(gc, wt["k"], "k")
            proj_v(gc)
            if gc >= 1:
                for sl in range(C):
                    for c in range(8):
                        scan_unit(gc - 1, sl, c)
                    m_chain((gc - 1) * C + sl)
        for sl in range(C):
            for c in range(8):
                scan_unit(NCH - 1, sl, c)
            m_chain((NCH - 1) * C + sl)
        if NCH >= 2:
            wo_chunk(NCH - 2)
        wo_chunk(NCH - 1)

        if do_ffn:
            for idx in range(NSA, NSA + 24):
                w_transpose(idx)

            for f in range(NFF):
                E.wait(te, "mix", ("f", f))
                for oc in range(8):
                    use_bank(oc % 2, [("kre", ("e", f, oc))])
                    i = None
                    for ic in range(8):
                        i = te.matmul(PRJ[oc % 2][:, 0:FFT * 128],
                                      wt["k"][:, ic * 1024 + oc * 128:ic * 1024 + oc * 128 + 128],
                                      xkfm[f % 2][:, ic * FFT * 128:(ic + 1) * FFT * 128],
                                      start=(ic == 0), stop=(ic == 7))
                    E.fire("kpe", ("t", f, oc), i)
                E.fire("kpe", ("f", f))
                for t in range(FFT):
                    ft = f * FFT + t
                    for h2 in range(2):
                        use_bank(h2, [("fsg", ("h", ft, h2))])
                        i = None
                        for ic in range(8):
                            i = te.matmul(PRJ[h2][:, 0:512],
                                          xrfm[f % 2][:, ic * FFT * 128 + t * 128:ic * FFT * 128 + (t + 1) * 128],
                                          wt["q"][:, ic * 1024 + h2 * 512:ic * 1024 + (h2 + 1) * 512],
                                          start=(ic == 0), stop=(ic == 7))
                        E.fire("rpe", ("h", ft, h2), i)
                    E.fire("rpe", ("ft", ft))
                E.wait(te, "kae", ("f", f))
                for t in range(FFT):
                    ft = f * FFT + t
                    for h2 in range(2):
                        use_bank(h2, [("ffm", ("h", ft, h2))])
                        i = None
                        for ic in range(8):
                            i = te.matmul(PRJ[h2][:, 0:512],
                                          kact[f % 2][:, ic * FFT * 128 + t * 128:ic * FFT * 128 + (t + 1) * 128],
                                          wt["v"][:, ic * 1024 + h2 * 512:ic * 1024 + (h2 + 1) * 512],
                                          start=(ic == 0), stop=(ic == 7))
                        E.fire("vp2", ("h", ft, h2), i)
                    E.fire("vp2", ("ft", ft))

    def emit_scalar(sc):
        E.wait(sc, "mub", "all")
        i = sc.activation(brow[0:1, 16:32], brow[0:1, 0:16], AF.Exp, scale=-1.0)
        E.fire("bp1", 0, i)
        E.wait(sc, "bp2", 0)
        i = sc.activation(brow[0:1, 64:80], brow[0:1, 48:64], AF.Ln)
        E.fire("bp3", 0, i)

        for idx in range(NSA):
            wname, dst = WL_A[idx // 8]
            rr = idx % 8
            for ic in range(8):
                gseq = idx * 8 + ic
                E.wait(sc, "wtm", ("mm", gseq))
                i = sc.copy(dst[:, ic * 1024 + 128 * rr:ic * 1024 + 128 * (rr + 1)],
                            PRJ[gseq % 2][:, 0:128])
                E.fire("wtr", ("e", gseq), i)

        def ln1_act(gc):
            st = lnst[gc % 2]
            E.wait(sc, "ln1s", ("g", gc))
            sc.activation(st[:, 16:18], st[:, 14:16], AF.Ln, bias=cst[:, 0:1])
            i = sc.activation(st[:, 18:20], st[:, 16:18], AF.Exp, scale=-0.5)
            E.fire("ln1r", ("g", gc), i)
            E.wait(sc, "ln1n", ("g", gc))
            if gc >= 2:
                E.wait(sc, f"trx{gc % 2}", ("g", gc - 2))
            i = None
            for sl in range(C):
                i = sc.activation(xln1[gc % 2][:, sl * 1024:(sl + 1) * 1024],
                                  xch[gc % 2][:, sl * 1024:(sl + 1) * 1024], AF.Identity,
                                  bias=st[:, 20 + sl:21 + sl], scale=st[:, 18 + sl:19 + sl])
            E.fire("ln1", ("g", gc), i)

        def qk_evac(gc):
            for kind, dstb in [("q", qfm), ("k", kfm)]:
                for oc in range(8):
                    E.wait(sc, "qpe", ("t", gc, kind, oc))
                    if gc >= 2:
                        E.wait(sc, "dmm", ("u", uidx(gc - 2, C - 1, 7)))
                    i = sc.copy(dstb[gc % 2][:, oc * CT:(oc + 1) * CT], PRJ[oc % 2][:, 0:CT])
                    E.fire("qke", ("e", gc, kind, oc), i)
            E.fire("qke", ("g", gc))

        def sig_exp(gc):
            E.wait(sc, "qke", ("g", gc))
            if gc >= 2:
                E.wait(sc, "dmm", ("u", uidx(gc - 2, C - 1, 7)))
                E.wait(sc, "mzs", ("tz", (gc - 2) * C + C - 1))
            sc.activation(sqf[gc % 2][:], qfm[gc % 2][:], AF.Exp)
            i = sc.activation(skf[gc % 2][:], kfm[gc % 2][:], AF.Exp)
            E.fire("sge", ("g", gc), i)

        def unit_act(gc, sl, c):
            u = uidx(gc, sl, c)
            E.wait(sc, "avm", ("u", u))
            if u >= 2:
                E.wait(sc, "adm", ("u", u - 2))
            i = None
            for par in range(2):
                i = sc.activation(ptsb[u % 2][:, 128 * par:128 * (par + 1)],
                                  pt_ap(u, par), AF.Exp, scale=0.125,
                                  bias=bcast[:, 16 + 2 * c + par:17 + 2 * c + par])
            E.fire("exp", ("u", u), i)
            E.wait(sc, "dnr", ("u", u))
            if u >= 2:
                E.wait(sc, "o2", ("u", u - 2))
            i = None
            for par in range(2):
                i = sc.activation(t1sb[u % 2][:, 64 * par:64 * (par + 1)],
                                  adot_ap(u, par)[:, 0:64], AF.Identity,
                                  scale=rsc[u % 2][:, 8 + par:9 + par])
            E.fire("o1", ("u", u), i)

        def tra_issue(gp):
            E.wait(sc, "o2", ("u", uidx(gp, C - 1, 7)))
            if gp >= 2:
                E.wait(sc, "wpe", ("chunk", gp - 2))
            for sl in range(C):
                for ic in range(8):
                    i = sc.dma_start(out=afm[gp % 2][:, fmcol(ic, sl):fmcol(ic, sl) + 128],
                                     in_=atm[gp % 2][:, sl * 1024 + ic * 128:sl * 1024 + (ic + 1) * 128],
                                     transpose=True)
                    E.fire(f"tra{gp % 2}", ("d", gp, sl, ic), i, dma=True)
            E.fire(f"tra{gp % 2}", ("g", gp))

        def wo_act(gw):
            for t in range(C):
                w = C * gw + t
                st = l2st[w % 2]
                E.wait(sc, "l2s", ("w", w))
                sc.activation(st[:, 8:9], st[:, 4:5], AF.Ln, bias=cst[:, 0:1])
                i = sc.activation(st[:, 9:10], st[:, 8:9], AF.Exp, scale=-0.5)
                E.fire("l2r", ("w", w), i)
                E.wait(sc, "l2n", ("w", w))
                if w >= 2:
                    E.wait(sc, "rsd", ("w", w - 2))
                i = None
                for h2 in range(2):
                    i = sc.activation(aosc[w % 2][:, 512 * h2:512 * (h2 + 1)],
                                      PRJ[h2][:, 0:512], AF.Identity,
                                      bias=st[:, 10:11], scale=st[:, 9:10])
                E.fire("aoc", ("w", w), i)

        for gc in range(NCH):
            if gc >= 2:
                wo_act(gc - 2)
            ln1_act(gc)
            qk_evac(gc)
            sig_exp(gc)
            if gc >= 1:
                for sl in range(C):
                    for c in range(8):
                        unit_act(gc - 1, sl, c)
                tra_issue(gc - 1)
        for sl in range(C):
            for c in range(8):
                unit_act(NCH - 1, sl, c)
        tra_issue(NCH - 1)
        if NCH >= 2:
            wo_act(NCH - 2)
        wo_act(NCH - 1)

        if do_ffn:
            for idx in range(NSA, NSA + 24):
                wname, dst = WL_F[(idx - NSA) // 8]
                rr = idx % 8
                for ic in range(8):
                    gseq = idx * 8 + ic
                    E.wait(sc, "wtm", ("mm", gseq))
                    i = sc.copy(dst[:, ic * 1024 + 128 * rr:ic * 1024 + 128 * (rr + 1)],
                                PRJ[gseq % 2][:, 0:128])
                    E.fire("wtr", ("e", gseq), i)
            for f in range(NFF):
                st = f4st[f % 2]
                E.wait(sc, "f3s", ("f", f))
                sc.activation(st[:, 16:18], st[:, 14:16], AF.Ln, bias=cst[:, 0:1])
                i = sc.activation(st[:, 18:20], st[:, 16:18], AF.Exp, scale=-0.5)
                E.fire("f3r", ("f", f), i)
                E.wait(sc, "f3n", ("f", f))
                if f >= 2:
                    E.wait(sc, "trl", ("f", f - 2))
                else:
                    E.wait(sc, f"trx{(NCH - 2 + (f % 2)) % 2}", ("g", NCH - 2 + (f % 2)))
                i = None
                for t in range(FFT):
                    i = sc.activation(xln3[f % 2][:, t * 1024:(t + 1) * 1024],
                                      x2f[f % 2][:, t * 1024:(t + 1) * 1024], AF.Identity,
                                      bias=st[:, 20 + t:21 + t], scale=st[:, 18 + t:19 + t])
                E.fire("ln3", ("f", f), i)
                E.wait(sc, "ln3", ("f", f))
                if f >= 1:
                    E.wait(sc, "mix", ("f", f - 1))
                for t in range(FFT):
                    for ic in range(8):
                        i = sc.dma_start(
                            out=x3fm_s[:, ic * 272 + 16 + t * 128:ic * 272 + 16 + (t + 1) * 128],
                            in_=xln3[f % 2][:, t * 1024 + ic * 128:t * 1024 + (ic + 1) * 128],
                            transpose=True)
                        E.fire("trl", ("d", f, t, ic), i, dma=True)
                E.fire("trl", ("f", f))
                for oc in range(8):
                    E.wait(sc, "kpe", ("t", f, oc))
                    if f >= 2:
                        E.wait(sc, "vp2", ("ft", (f - 2) * FFT + FFT - 1))
                    else:
                        E.wait(sc, "adm", ("u", NU - 1))
                    i = sc.activation(kact[f % 2][:, oc * FFT * 128:(oc + 1) * FFT * 128],
                                      PRJ[oc % 2][:, 0:FFT * 128], AF.Relu)
                    E.fire("kre", ("e", f, oc), i)
                E.fire("kre", ("f", f))
                for t in range(FFT):
                    ft = f * FFT + t
                    if f == 0 and t == 0:
                        E.wait(sc, "mzs", ("t", NST - 1))
                        E.wait(sc, "gum", ("t", NST - 2, 1))
                        E.wait(sc, f"mbl{(NST - 1) % 2}", ("t", NST - 1))
                    if ft >= FFT:
                        E.wait(sc, "rev", ("ft", ft - FFT))
                    last = None
                    for h2 in range(2):
                        E.wait(sc, "rpe", ("h", ft, h2))
                        last = sc.activation(fsig[f % 2][:, t * 1024 + h2 * 512:t * 1024 + (h2 + 1) * 512],
                                             PRJ[h2][:, 0:512], AF.Exp, scale=-1.0)
                    E.fire("fsg", ("ft", ft), last)
                    E.fire("fsg", ("h", ft, 0))
                    E.fire("fsg", ("h", ft, 1))
                for t in range(FFT):
                    ft = f * FFT + t
                    E.wait(sc, "f4s", ("ft", ft))
                    sc.activation(st[:, 28 + t:29 + t], st[:, 26 + t:27 + t], AF.Ln, bias=cst[:, 0:1])
                    i = sc.activation(st[:, 30 + t:31 + t], st[:, 28 + t:29 + t], AF.Exp, scale=-0.5)
                    E.fire("f4r", ("ft", ft), i)

    def emit_vector(v):
        v.memset(bsrc[0:1, :], 0.0)
        v.memset(cst[:, 0:1], EPS_LN)
        v.memset(zrow[0:1, :], 0.0)
        v.memset(mz2[:], 0.0)
        v.memset(zmast[:], 0.0)
        v.memset(zbd[:], 0.0)
        for i_ in range(2):
            for b_ in range(2):
                v.memset(sks[i_][b_][:], 0.0)
        E.wait(v, "bp1", 0)
        v.tensor_scalar(brow[0:1, 32:48], brow[0:1, 16:32], 1.0, None, ALU.add)
        v.reciprocal(brow[0:1, 32:48], brow[0:1, 32:48])
        i = v.tensor_tensor(brow[0:1, 48:64], brow[0:1, 16:32], brow[0:1, 32:48], ALU.mult)
        E.fire("bp2", 0, i)
        E.wait(v, "bp3", 0)
        v.reciprocal(brow[0:1, 80:96], brow[0:1, 48:64])
        v.tensor_copy(bsrc[0:1, 0:16], brow[0:1, 32:48])
        v.tensor_copy(bsrc[0:1, 16:32], brow[0:1, 64:80])
        i = v.tensor_copy(bsrc[0:1, 32:48], brow[0:1, 80:96])
        E.fire("bp4", 0, i)
        E.wait(v, "bt5", "done")
        i = None
        for ii in range(2):
            i = v.tensor_copy(
                vtm[ii][:].rearrange("p (s c q e) -> p s c q e", s=C, c=8, q=2)[:, :, :, :, 64],
                bcast[:, 32:48].rearrange("p (c q) -> p c q", q=2)
                .unsqueeze(1).broadcast_to([128, C, 8, 2]))
        E.fire("bet", 0, i)


        def stats4(st, src, mean_c, var_c, tmpa, tmpb):
            v.bn_stats(st[:, 0:6], src[:, 0:512])
            v.bn_stats(st[:, 6:12], src[:, 512:1024])
            m4 = st[:, 1:12:3]
            cv4 = st[:, 2:12:3]
            v.reduce_sum(st[:, mean_c:mean_c + 1], m4, axis=AX)
            v.tensor_scalar(st[:, mean_c:mean_c + 1], st[:, mean_c:mean_c + 1], 0.25, None, ALU.mult)
            v.tensor_tensor(m4, m4, m4, ALU.mult)
            v.reduce_sum(st[:, tmpa:tmpa + 1], m4, axis=AX)
            v.tensor_scalar(st[:, tmpa:tmpa + 1], st[:, tmpa:tmpa + 1], 0.25, None, ALU.mult)
            v.reduce_sum(st[:, tmpb:tmpb + 1], cv4, axis=AX)
            v.tensor_scalar(st[:, tmpb:tmpb + 1], st[:, tmpb:tmpb + 1], 1.0 / 1024.0, None, ALU.mult)
            v.tensor_tensor(st[:, tmpa:tmpa + 1], st[:, tmpa:tmpa + 1], st[:, tmpb:tmpb + 1], ALU.add)
            v.tensor_tensor(st[:, tmpb:tmpb + 1], st[:, mean_c:mean_c + 1], st[:, mean_c:mean_c + 1], ALU.mult)
            return v.tensor_tensor(st[:, var_c:var_c + 1], st[:, tmpa:tmpa + 1], st[:, tmpb:tmpb + 1], ALU.subtract)

        def ln1_stats(gc):
            st = lnst[gc % 2]
            E.wait(v, f"xin{gc % 2}", ("chunk", gc))
            if gc >= 2:
                E.wait(v, "ln1", ("g", gc - 2))
            for sl in range(C):
                src = xch[gc % 2][:, sl * 1024:(sl + 1) * 1024]
                v.reduce_sum(st[:, 12 + sl:13 + sl], src, axis=AX)
                v.tensor_tensor(sqscr, src, src, ALU.mult)
                v.reduce_sum(st[:, 22 + sl:23 + sl], sqscr, axis=AX)
            v.tensor_scalar(st[:, 12:14], st[:, 12:14], 1.0 / 1024.0, None, ALU.mult)
            v.tensor_scalar(st[:, 22:24], st[:, 22:24], 1.0 / 1024.0, None, ALU.mult)
            v.tensor_tensor(st[:, 16:18], st[:, 12:14], st[:, 12:14], ALU.mult)
            i = v.tensor_tensor(st[:, 14:16], st[:, 22:24], st[:, 16:18], ALU.subtract)
            E.fire("ln1s", ("g", gc), i)
            E.wait(v, "ln1r", ("g", gc))
            i = v.scalar_tensor_tensor(st[:, 20:22], st[:, 12:14], -1.0, st[:, 18:20],
                                       ALU.mult, ALU.mult)
            E.fire("ln1n", ("g", gc), i)

        def sig_dve(gc):
            E.wait(v, "qke", ("g", gc))
            E.wait(v, "sge", ("g", gc))
            v.tensor_scalar(sqf[gc % 2][:], sqf[gc % 2][:], 1.0, None, ALU.min)
            v.scalar_tensor_tensor(sqf[gc % 2][:], qfm[gc % 2][:], 0.0, sqf[gc % 2][:],
                                   ALU.max, ALU.add)
            v.tensor_scalar(skf[gc % 2][:], skf[gc % 2][:], 1.0, None, ALU.min)
            i = v.scalar_tensor_tensor(skf[gc % 2][:], kfm[gc % 2][:], 0.0, skf[gc % 2][:],
                                       ALU.max, ALU.add)
            E.fire("sig", ("g", gc), i)

        def v_evac(gc):
            for sl in range(C):
                for h2 in range(2):
                    j = 2 * sl + h2
                    E.wait(v, "qpe", ("v", gc, j))
                    if gc >= 2:
                        E.wait(v, "dmm", ("u", uidx(gc - 2, C - 1, 7)))
                    i = v.tensor_copy(
                        vtm[gc % 2][:].rearrange("p (s c q e) -> p s c q e", s=C, c=8, q=2)
                        [:, sl, 4 * h2:4 * (h2 + 1), :, 0:64],
                        PRJ[h2][:, 0:512].rearrange("p (c q e) -> p c q e", q=2, e=64))
                    E.fire("vev", ("e", gc, j), i)
            E.fire("vev", ("g", gc))

        def unit_dve(gc, sl, c):
            u = uidx(gc, sl, c)
            t = gc * C + sl
            ring = gc % 2
            rs = rsc[u % 2]
            last = (t == NST - 1)
            E.wait(v, "exp", ("u", u))
            i = v.tensor_tensor(ptsb[u % 2][:].rearrange("p (q n) -> p q n", q=2),
                                ptsb[u % 2][:].rearrange("p (q n) -> p q n", q=2),
                                mask01[:].unsqueeze(1).broadcast_to([128, 2, 128]), ALU.mult)
            E.fire("msk", ("u", u), i)
            E.wait(v, "avm", ("u", u))
            if u >= 2:
                E.wait(v, "o1", ("u", u - 2))
                E.wait(v, "adm", ("u", u - 2), optional=True)
            if t >= 1 and not last and "gcon" not in ablate and "sks" not in ablate:
                E.wait(v, f"trk{gc % 2}", ("g", gc))
                v.tensor_scalar(rs[:, 0:4], denk_ap(u), EPS_A, -1.0, ALU.add, ALU.mult)
                v.reciprocal(rs[:, 0:4], rs[:, 0:4])
                for half in range(2):
                    v.tensor_tensor(
                        sks[u % 2][half][64 * half:64 * (half + 1), :]
                        .rearrange("p (q e) -> p q e", e=64),
                        sktm[ring][64 * half:64 * (half + 1),
                                   sl * 1024 + c * 128:sl * 1024 + (c + 1) * 128]
                        .rearrange("p (q e) -> p q e", e=64),
                        rs[64 * half:64 * (half + 1), 2 * half:2 * half + 2]
                        .unsqueeze(2).broadcast_to([64, 2, 64]),
                        ALU.mult)
            v.tensor_scalar(rs[:, 4:6], amem_ap(u, 0)[:, 128:130], EPS_A, None, ALU.add)
            v.tensor_scalar(rs[:, 6:8], amem_ap(u, 1)[:, 128:130], EPS_A, None, ALU.add)
            v.reciprocal(rs[:, 4:8], rs[:, 4:8])
            v.tensor_tensor(rs[:, 4:8].rearrange("p (b q) -> p b q", b=2),
                            rs[:, 4:8].rearrange("p (b q) -> p b q", b=2),
                            bcast[:, 2 * c:2 * c + 2].unsqueeze(1).broadcast_to([128, 2, 2]),
                            ALU.mult)
            i = None
            for b in range(2):
                i = v.tensor_tensor(
                    t2sb[u % 2][64 * b:64 * (b + 1), :].rearrange("p (q e) -> p q e", e=64),
                    amem_ap(u, b)[64 * b:64 * (b + 1), 0:128].rearrange("p (q e) -> p q e", e=64),
                    rs[64 * b:64 * (b + 1), 4 + 2 * b:6 + 2 * b]
                    .unsqueeze(2).broadcast_to([64, 2, 64]),
                    ALU.mult)
            E.fire("p1d", ("u", u), i)
            E.wait(v, "adm", ("u", u))
            v.reciprocal(rs[:, 8:9], adot_ap(u, 0)[:, 64:65])
            i = v.reciprocal(rs[:, 9:10], adot_ap(u, 1)[:, 64:65])
            E.fire("dnr", ("u", u), i)
            E.wait(v, "o1", ("u", u))
            if t >= 1 and not last and "gcon" not in ablate and "gev" not in ablate:
                if t >= 3:
                    E.wait(v, "gum", ("t", t - 2, 1))
                ring2 = (t % 2) * 1024
                for b in range(2):
                    for par in range(2):
                        i = v.tensor_copy(
                            gsb[64 * b:64 * (b + 1),
                                ring2 + c * 128 + par * 64:ring2 + c * 128 + (par + 1) * 64],
                            _slot(u, b)[64 * b:64 * (b + 1), 323 + 64 * par:387 + 64 * par])
            E.fire("gev", ("u", u), i)
            if u >= 2:
                E.wait(v, f"tra{gc % 2}", ("g", gc - 2), optional=True)
            i = v.tensor_tensor(
                atm[ring][:, sl * 1024 + c * 128:sl * 1024 + (c + 1) * 128],
                t2sb[u % 2][:], t1sb[u % 2][:], ALU.add)
            E.fire("o2", ("u", u), i)

        def m_evac(t):
            E.wait(v, "dmm", ("u", uidx((t - 1) // C, (t - 1) % C, 7)))
            if t >= 2:
                E.wait(v, "gum", ("t", t - 1, 1), optional=True)
                E.wait(v, "mbs", ("t", t - 2), optional=True)
            ring = (t % 2) * 1024
            for half in range(2):
                i = v.tensor_copy(mzsb[:, ring + half * 512:ring + (half + 1) * 512],
                                  MBK[half][:, 0:512])
                E.fire("mze", ("t", t, half), i)
            E.fire("mze", ("t", t))

        def conv(t):
            E.wait(v, "avm", ("u", uidx((t - 1) // C, (t - 1) % C, 7)))
            ring = (t % 2) * 1024
            mzq = mzsb[:, ring:ring + 1024].rearrange("p (c q) -> p c q", q=128)
            mzv0 = mz2[0:64, :].rearrange("p (c two n) -> p c two n", two=2, n=130)
            mzv1 = mz2[64:128, :].rearrange("p (c two n) -> p c two n", two=2, n=130)
            v.tensor_copy(mzv0[:, :, 0, 0:64], mzq[0:64, :, 0:64])
            v.tensor_copy(mzv1[:, :, 1, 64:128], mzq[64:128, :, 64:128])
            v.tensor_copy(mz2[0:64, :].rearrange("p (cb n) -> p cb n", n=130)[:, :, 128],
                          zmast[0:64, :])
            v.tensor_copy(mz2[64:128, :].rearrange("p (cb n) -> p cb n", n=130)[:, :, 129],
                          zmast[64:128, :])
            zv = zbd[:, (t % 2) * 32:(t % 2) * 32 + 32].rearrange(
                "p (c b q) -> p c b q", b=2, q=2)
            v.tensor_copy(zv[0:64, :, :, 0], zmast[0:64, :].rearrange("p (c b) -> p c b", b=2))
            i = v.tensor_copy(zv[64:128, :, :, 1],
                              zmast[64:128, :].rearrange("p (c b) -> p c b", b=2))
            E.fire("mzs", ("t", t), i)

        def zinc_step(gc, sl):
            t = gc * C + sl
            ring = gc % 2
            v.reduce_sum(zinc[:].rearrange("p (c b) -> p c b", b=2),
                         skf[ring][:].rearrange("p (ic s bj) -> p ic s bj", ic=8, s=C)
                         [:, :, sl, :].rearrange("p ic (b j) -> p ic b j", b=2), axis=AX)
            i = v.tensor_tensor(zmast[:], zmast[:], zinc[:], ALU.add)
            E.fire("mzs", ("tz", t), i)

        def wo_dve(gw):
            for t in range(C):
                w = C * gw + t
                st = l2st[w % 2]
                E.wait(v, "wpe", ("w", w))
                if w >= 2:
                    E.wait(v, "rsd", ("w", w - 2))
                v.bn_stats(st[:, 0:6], PRJ[0][:, 0:512])
                v.bn_stats(st[:, 6:12], PRJ[1][:, 0:512])
                m4 = st[:, 1:12:3]
                cv4 = st[:, 2:12:3]
                v.reduce_sum(st[:, 12:13], m4, axis=AX)
                v.tensor_scalar(st[:, 12:13], st[:, 12:13], 0.25, None, ALU.mult)
                v.tensor_tensor(m4, m4, m4, ALU.mult)
                v.reduce_sum(st[:, 13:14], m4, axis=AX)
                v.tensor_scalar(st[:, 13:14], st[:, 13:14], 0.25, None, ALU.mult)
                v.reduce_sum(st[:, 14:15], cv4, axis=AX)
                v.tensor_scalar(st[:, 14:15], st[:, 14:15], 1.0 / 1024.0, None, ALU.mult)
                v.tensor_tensor(st[:, 13:14], st[:, 13:14], st[:, 14:15], ALU.add)
                v.tensor_tensor(st[:, 14:15], st[:, 12:13], st[:, 12:13], ALU.mult)
                i = v.tensor_tensor(st[:, 4:5], st[:, 13:14], st[:, 14:15], ALU.subtract)
                E.fire("l2s", ("w", w), i)
                E.wait(v, "l2r", ("w", w))
                i = v.scalar_tensor_tensor(st[:, 10:11], st[:, 12:13],
                                           -1.0, st[:, 9:10], ALU.mult, ALU.mult)
                E.fire("l2n", ("w", w), i)
                E.wait(v, "aoc", ("w", w))
                E.wait(v, f"x2o{gw % 2}", ("chunk", gw - 2), optional=True)
                i = v.tensor_tensor(
                    x2ch[gw % 2][:, t * 1024:(t + 1) * 1024],
                    aosc[w % 2][:],
                    xch[gw % 2][:, t * 1024:(t + 1) * 1024], ALU.add)
                E.fire("rsd", ("w", w), i)

        for gc in range(NCH):
            if gc >= 2:
                wo_dve(gc - 2)
            ln1_stats(gc)
            v_evac(gc)
            sig_dve(gc)
            if gc >= 1:
                for sl in range(C):
                    t = (gc - 1) * C + sl
                    if t >= 1:
                        m_evac(t)
                        conv(t)
                    zinc_step(gc - 1, sl)
                    for c in range(8):
                        unit_dve(gc - 1, sl, c)
        for sl in range(C):
            t = (NCH - 1) * C + sl
            m_evac(t)
            conv(t)
            zinc_step(NCH - 1, sl)
            for c in range(8):
                unit_dve(NCH - 1, sl, c)
        if NCH >= 2:
            wo_dve(NCH - 2)
        wo_dve(NCH - 1)

        if do_ffn:
            E.wait(v, "adm", ("u", NU - 1))
            E.wait(v, f"trk{(NCH - 1) % 2}", ("g", NCH - 1))
            E.wait(v, "gum", ("t", NST - 2, 1))
            for f in range(NFF):
                st = f4st[f % 2]
                E.wait(v, f"x2i{f % 2}", ("f", f))
                if f >= 2:
                    E.wait(v, "ln3", ("f", f - 2))
                for t in range(FFT):
                    src = x2f[f % 2][:, t * 1024:(t + 1) * 1024]
                    v.reduce_sum(st[:, 12 + t:13 + t], src, axis=AX)
                    v.tensor_tensor(sqscr, src, src, ALU.mult)
                    v.reduce_sum(st[:, 22 + t:23 + t], sqscr, axis=AX)
                v.tensor_scalar(st[:, 12:14], st[:, 12:14], 1.0 / 1024.0, None, ALU.mult)
                v.tensor_scalar(st[:, 22:24], st[:, 22:24], 1.0 / 1024.0, None, ALU.mult)
                v.tensor_tensor(st[:, 24:26], st[:, 12:14], st[:, 12:14], ALU.mult)
                i = v.tensor_tensor(st[:, 14:16], st[:, 22:24], st[:, 24:26], ALU.subtract)
                E.fire("f3s", ("f", f), i)
                E.wait(v, "f3r", ("f", f))
                i = v.scalar_tensor_tensor(st[:, 20:22], st[:, 12:14], -1.0, st[:, 18:20],
                                           ALU.mult, ALU.mult)
                E.fire("f3n", ("f", f), i)
                E.wait(v, "trl", ("f", f))
                if f >= 2:
                    E.wait(v, "kpe", ("f", f - 2))
                    E.wait(v, "rpe", ("ft", (f - 2) * FFT + FFT - 1))
                else:
                    E.wait(v, "adm", ("u", NU - 1))
                x3v = x3fm_s[:].rearrange("p (ic n) -> p ic n", n=272)
                if f % 8 == 0:
                    v.memset(x3v[:, :, 15], 0.0)
                else:
                    v.tensor_copy(x3v[:, :, 15], csv[:])
                dv = scrb[f % 2][:].rearrange("p (ic n) -> p ic n", n=256)
                xrv = xrfm[f % 2][:].rearrange("p (ic n) -> p ic n", n=256)
                xkv = xkfm[f % 2][:].rearrange("p (ic n) -> p ic n", n=256)
                xprev = x3v[:, :, 15:271]
                v.tensor_tensor(dv, x3v[:, :, 16:272], xprev, ALU.subtract)
                v.tensor_tensor(xrv, dv,
                                mu_fm[:, 0:8].unsqueeze(2).broadcast_to([128, 8, 256]),
                                ALU.mult)
                v.tensor_tensor(xrv, xrv, xprev, ALU.add)
                v.tensor_tensor(xkv, dv,
                                mu_fm[:, 8:16].unsqueeze(2).broadcast_to([128, 8, 256]),
                                ALU.mult)
                v.tensor_tensor(xkv, xkv, xprev, ALU.add)
                i = v.tensor_copy(csv[:], x3v[:, :, 271])
                E.fire("mix", ("f", f), i)
                E.wait(v, "kre", ("f", f))
                i = v.tensor_tensor(kact[f % 2][:], kact[f % 2][:], kact[f % 2][:], ALU.mult)
                E.fire("kae", ("f", f), i)
                for t in range(FFT):
                    ft = f * FFT + t
                    sF = f4st[f % 2]
                    E.wait(v, "fsg", ("ft", ft))
                    v.tensor_scalar(rtm[f % 2][:, t * 1024:(t + 1) * 1024],
                                    fsig[f % 2][:, t * 1024:(t + 1) * 1024], 1.0, None, ALU.add)
                    with nc.allow_low_precision(reason="sigmoid output in bf16 (r in (0,1))"):
                        i = v.reciprocal(rtm[f % 2][:, t * 1024:(t + 1) * 1024],
                                         rtm[f % 2][:, t * 1024:(t + 1) * 1024])
                    E.fire("rev", ("ft", ft), i)
                    if ft < 2:
                        E.wait(v, "rsd", ("w", NWT - 2 + ft))
                    for h2 in range(2):
                        E.wait(v, "vp2", ("h", ft, h2))
                        i = v.tensor_tensor(
                            fsc[ft % 2][:, 512 * h2:512 * (h2 + 1)],
                            PRJ[h2][:, 0:512],
                            rtm[f % 2][:, t * 1024 + 512 * h2:t * 1024 + 512 * (h2 + 1)],
                            ALU.mult)
                        E.fire("ffm", ("h", ft, h2), i)
                    E.fire("ffm", ("ft", ft))
                    v.reduce_sum(sF[:, 10:11], fsc[ft % 2][:], axis=AX)
                    v.tensor_scalar(sF[:, 24:25], sF[:, 10:11], 1.0 / 1024.0, None, ALU.mult)
                    v.tensor_tensor(sqscr, fsc[ft % 2][:], fsc[ft % 2][:], ALU.mult)
                    v.reduce_sum(sF[:, 11:12], sqscr, axis=AX)
                    v.tensor_scalar(sF[:, 11:12], sF[:, 11:12], 1.0 / 1024.0, None, ALU.mult)
                    v.tensor_tensor(sF[:, 10:11], sF[:, 24:25], sF[:, 24:25], ALU.mult)
                    i = v.tensor_tensor(sF[:, 26 + t:27 + t], sF[:, 11:12], sF[:, 10:11],
                                        ALU.subtract)
                    E.fire("f4s", ("ft", ft), i)
                    E.wait(v, "f4r", ("ft", ft))
                    if f >= 2:
                        lastt = (f - 2) * FFT + FFT - 1
                        E.wait(v, f"out{((f - 2)) % 2}", ("ft", lastt))
                    else:
                        alias_g = NCH - 2 + (f % 2)
                        E.wait(v, f"x2o{alias_g % 2}", ("chunk", alias_g))
                    v.scalar_tensor_tensor(sF[:, 25:26], sF[:, 24:25], -1.0, sF[:, 30 + t:31 + t],
                                           ALU.mult, ALU.mult)
                    v.tensor_scalar(fout[f % 2][:, 1024 * t:1024 * (t + 1)],
                                    x2f[f % 2][:, t * 1024:(t + 1) * 1024], sF[:, 25:26], None, ALU.add)
                    i = v.scalar_tensor_tensor(
                        fout[f % 2][:, 1024 * t:1024 * (t + 1)],
                        fsc[ft % 2][:], sF[:, 30 + t:31 + t],
                        fout[f % 2][:, 1024 * t:1024 * (t + 1)],
                        ALU.mult, ALU.add)
                    E.fire("ffo", ("ft", ft), i)

    class _MockInstr:
        def then_inc(self, s, n):
            return self

    class _Mock:
        def __getattr__(self, k):
            return lambda *a, **kw: _MockInstr()

    mock = _Mock()
    emit_gpsimd(mock)
    emit_sync(mock)
    emit_tensor(mock)
    emit_scalar(ChainEng(mock, E, "act"))
    emit_vector(ChainEng(mock, E, "dve"))
    _count_chains = dict(E.chains)
    _count_counts = dict(E.counts)
    E.reset()

    with nc.Block() as block:
        @block.gpsimd
        def _(g):
            emit_gpsimd(g)

        @block.sync
        def _(s):
            emit_sync(s)

        @block.tensor
        def _(te):
            emit_tensor(te)

        @block.scalar
        def _(sc):
            emit_scalar(ChainEng(sc, E, "act"))

        @block.vector
        def _(v):
            emit_vector(ChainEng(v, E, "dve"))

    assert E.chains == _count_chains, ("chain mismatch", E.chains, _count_chains)
    assert E.counts == _count_counts, ("count mismatch")

    ctx.close()
    return nc, dbg


_MASK = None


def _consts():
    global _MASK
    if _MASK is None:
        m = np.zeros((128, 128), np.float32)
        for b in range(2):
            blk = np.tril(np.ones((64, 64), np.float32))
            m[64 * b:64 * (b + 1), 64 * b:64 * (b + 1)] = blk.T
        _MASK = m
    return _MASK, np.eye(128, dtype=np.float32)


def kernel(L0_attn_x, L0_attn_W_q, L0_attn_W_k, L0_attn_W_v, L0_attn_W_o,
           L0_attn_beta_param, L0_ffn_W_r, L0_ffn_W_k, L0_ffn_W_v,
           L0_ffn_mu_r, L0_ffn_mu_k):
    x = np.ascontiguousarray(np.asarray(L0_attn_x, dtype=np.float32))
    mask, iden = _consts()
    nc, _ = build_nc()
    in_maps = []
    for core in range(8):
        in_maps.append({
            "x": np.ascontiguousarray(x[2 * core:2 * core + 2]),
            "wq": np.asarray(L0_attn_W_q, np.float32),
            "wk": np.asarray(L0_attn_W_k, np.float32),
            "wv": np.asarray(L0_attn_W_v, np.float32),
            "wo": np.asarray(L0_attn_W_o, np.float32),
            "wr": np.asarray(L0_ffn_W_r, np.float32),
            "wfk": np.asarray(L0_ffn_W_k, np.float32),
            "wfv": np.asarray(L0_ffn_W_v, np.float32),
            "beta": np.asarray(L0_attn_beta_param, np.float32).reshape(1, 16),
            "mur": np.asarray(L0_ffn_mu_r, np.float32).reshape(1, D),
            "muk": np.asarray(L0_ffn_mu_k, np.float32).reshape(1, D),
            "mask01": mask,
            "iden": iden,
        })
    res = run_bass_kernel_spmd(nc, in_maps, core_ids=list(range(8)))
    return np.concatenate([r["out"] for r in res.results], axis=0)
